# version 18
# baseline (speedup 1.0000x reference)
import numpy as np
import ml_dtypes
from contextlib import ExitStack

import concourse.bass as bass
import concourse.mybir as mybir
from concourse.bass_utils import run_bass_kernel_spmd

F32 = mybir.dt.float32
BF16 = mybir.dt.bfloat16
AF = mybir.ActivationFunctionType
ALU = mybir.AluOpType
AX = mybir.AxisListType

C = 4096
NC = 32
H = 32
TB = 4096
NG = 4
GW = 1024
OWN = 2048
QW = 512
NEG = -32768.0
EXPB = -20.0
S128 = 1.0 / float(np.sqrt(128.0))
S64 = 1.0 / 64.0

ENGS = ("sync", "gpsimd", "tensor", "vector", "scalar")


class Sched:

    def __init__(self, nc, es):
        self.nc = nc
        self.sems = {}
        self.es = es
        self.dma_sems = set()

    def sem(self, name):
        if name not in self.sems:
            self.sems[name] = [self.es.enter_context(self.nc.semaphore(name)), 0]
        return self.sems[name]

    def n(self, name):
        return self.sem(name)[1]

    def new_block(self):
        self.ops = {e: [] for e in ENGS}
        self._w = []

    def w(self, name, val=None):
        s = self.sem(name)
        self._w.append((s[0], s[1] if val is None else val))
        return self

    def op(self, eng, fn, inc=None, k=1):
        waits = self._w
        self._w = []
        tgt = None
        if inc is not None:
            s = self.sem(inc)
            s[1] += k
            tgt = (s[0], k)
        self.ops[eng].append((waits, fn, tgt))
        return self.n(inc) if inc else None

    def dma(self, eng, inc, out, in_):
        self.dma_sems.add(inc)
        return self.op(eng, lambda q, o=out, i=in_: q.dma_start(out=o, in_=i),
                       inc=inc, k=16)

    def fence_dma(self):
        for e in ENGS:
            for name in sorted(self.dma_sems):
                self.w(name, self.n(name))
            for name in ("aa", "vv", "tensor"):
                if name in self.sems:
                    self.w(name, self.n(name))
            self.op(e, lambda q: q.nop())

    def emit(self):
        with self.nc.Block() as block:
            for e in ENGS:
                if not self.ops[e]:
                    continue
                sec = getattr(block, e)

                def emit_section(q, _ops=self.ops[e]):
                    for waits, fn, tgt in _ops:
                        for (sh, v) in waits:
                            q.wait_ge(sh, v)
                        inst = fn(q)
                        if tgt is not None:
                            inst.then_inc(tgt[0], tgt[1])
                sec(emit_section)


def build_program():
    nc = bass.Bass("TRN2", target_bir_lowering=False)

    d_xp = nc.dram_tensor("xp", [TB, C], F32, kind="ExternalInput").ap()
    d_wqkT = nc.dram_tensor("wqkT", [C, 3 * C], F32, kind="ExternalInput").ap()
    d_wpT = nc.dram_tensor("wpT", [C, C], F32, kind="ExternalInput").ap()
    d_wfT = nc.dram_tensor("wfT", [C, C], F32, kind="ExternalInput").ap()
    d_awT = nc.dram_tensor("awT", [C, 10], F32, kind="ExternalInput").ap()
    d_AT = nc.dram_tensor("AT", [C, 64], F32, kind="ExternalInput").ap()
    d_BT = nc.dram_tensor("BTm", [64, C], F32, kind="ExternalInput").ap()
    d_gwT = nc.dram_tensor("gwT", [C, 4], F32, kind="ExternalInput").ap()
    d_cosT = nc.dram_tensor("cosT", [128, TB], BF16, kind="ExternalInput").ap()
    d_srtT = nc.dram_tensor("srtT", [128, TB], BF16, kind="ExternalInput").ap()
    d_mq = nc.dram_tensor("mq", [4, 128, 512], F32, kind="ExternalInput").ap()
    d_hc = nc.dram_tensor("hc", [128, 2], F32, kind="ExternalInput").ap()
    d_gating = nc.dram_tensor("gating", [1, 1], F32, kind="ExternalInput").ap()
    d_gbbc = nc.dram_tensor("gbbc", [128, 4], F32, kind="ExternalInput").ap()
    d_epsio = nc.dram_tensor("epsio", [128, 4], F32, kind="ExternalInput").ap()
    d_rkm = nc.dram_tensor("rkm", [4, 64], F32, kind="ExternalInput").ap()
    d_bbc = nc.dram_tensor("bbc", [128, C], F32, kind="ExternalInput").ap()
    d_ident = nc.dram_tensor("identf", [128, 128], F32, kind="ExternalInput").ap()

    d_out = nc.dram_tensor("out", [OWN, C], F32, kind="ExternalOutput").ap()
    d_xn_tm = nc.dram_tensor("xn_tm", [2 * GW, C], BF16, kind="Internal").ap()
    d_xnf = nc.dram_tensor("xnf", [NC, 128, OWN], BF16, kind="Internal").ap()
    d_q = nc.dram_tensor("q_scr", [H, 128, OWN], BF16, kind="Internal").ap()
    d_k = nc.dram_tensor("k_scr", [H, 128, TB], BF16, kind="Internal").ap()
    d_v = nc.dram_tensor("v_scr", [H, 128, TB], BF16, kind="Internal").ap()
    d_acc = nc.dram_tensor("acc_scr", [OWN, C], F32, kind="Internal").ap()
    d_wbf = nc.dram_tensor("wbf_scr", [96, 128, C], BF16, kind="Internal").ap()
    d_wpbf = nc.dram_tensor("wpbf_scr", [8, 128, 32 * 512], BF16, kind="Internal").ap()
    d_wfbf = nc.dram_tensor("wfbf_scr", [8, 128, 32 * 512], BF16, kind="Internal").ap()

    es0 = ExitStack()
    S = Sched(nc, es0)

    ccol = es0.enter_context(nc.sbuf_tensor("ccol", [128, 8], F32))
    hc_sb = es0.enter_context(nc.sbuf_tensor("hc_sb", [128, 2], F32))
    gat_sb = es0.enter_context(nc.sbuf_tensor("gat_sb", [1, 1], F32))
    ones_b = es0.enter_context(nc.sbuf_tensor("ones_b", [128, 128], BF16))
    ones_f = es0.enter_context(nc.sbuf_tensor("ones_f", [1, 128], F32))
    idb = es0.enter_context(nc.sbuf_tensor("idb", [128, 128], BF16))
    akav = es0.enter_context(nc.sbuf_tensor("akav", [128, 64 * 10], BF16))
    awte = es0.enter_context(nc.sbuf_tensor("awte", [128, NC * 10], BF16))

    S.new_block()
    S.dma("sync", "c0", hc_sb[:], d_hc[:])
    S.dma("sync", "c0", gat_sb[:], d_gating[:])
    S.dma("gpsimd", "g0", idb[:], d_ident[:])
    S.dma("gpsimd", "g0", awte[:], d_awT.rearrange("(c p) j -> p c j", p=128))
    S.op("vector", lambda q: q.memset(ccol[:, 0:1], EXPB), inc="vv")
    S.op("vector", lambda q: q.memset(ccol[:, 1:2], S128), inc="vv")
    S.op("vector", lambda q: q.memset(ccol[:, 2:3], S64), inc="vv")
    S.op("vector", lambda q: q.memset(ccol[:, 3:4], 1e-5), inc="vv")
    S.op("vector", lambda q: q.memset(ones_b[:], 1.0), inc="vv")
    S.op("vector", lambda q: q.memset(ones_f[:], 1.0), inc="vv")
    S.w("c0", S.n("c0")).w("g0", S.n("g0"))
    S.op("sync", lambda q: q.nop())
    S.emit()

    es1 = ExitStack()
    xs = [es1.enter_context(nc.sbuf_tensor(f"xs{i}", [128, C], F32)) for i in range(2)]
    stats = es1.enter_context(nc.sbuf_tensor("stats", [128, 48], F32))
    mv = es1.enter_context(nc.sbuf_tensor("mv", [128, 2], F32))
    sdt = es1.enter_context(nc.sbuf_tensor("sdt", [128, 1], F32))
    rstd = es1.enter_context(nc.sbuf_tensor("rstd", [128, 1], F32))
    xstg = [es1.enter_context(nc.sbuf_tensor(f"xstg{i}", [128, C], BF16)) for i in range(2)]
    xnT = [es1.enter_context(nc.sbuf_tensor(f"xnT{c}", [128, GW], BF16)) for c in range(NC)]
    cosT = es1.enter_context(nc.sbuf_tensor("cosT_sb", [128, TB], BF16))
    srtT = es1.enter_context(nc.sbuf_tensor("srtT_sb", [128, TB], BF16))
    wb = [es1.enter_context(nc.sbuf_tensor(f"wb{i}", [128, C], BF16)) for i in range(2)]
    wb32 = [es1.enter_context(nc.sbuf_tensor(f"wb32_{i}", [128, C], F32)) for i in range(2)]
    kqs = [es1.enter_context(nc.sbuf_tensor(f"kqs{i}", [128, GW], BF16)) for i in range(2)]
    t1f = es1.enter_context(nc.sbuf_tensor("t1f", [128, GW], F32))
    t2f = es1.enter_context(nc.sbuf_tensor("t2f", [128, GW], F32))
    ps = [es1.enter_context(nc.psum_tensor(f"ps{i}", [128, GW], F32)) for i in range(2)]
    akp = [es1.enter_context(nc.psum_tensor(f"akp{i}", [128, 10], F32)) for i in range(2)]

    NJ = 96
    OORDER = list(range(32, 96)) + list(range(32))

    def wsrc(o):
        return d_wqkT.rearrange("(c p) o -> p c o", p=128)[:, :, o * 128:(o + 1) * 128]

    prev_job_end = [None, None]
    prev_epi = [None, None]
    lnst = {"prev": None, "done": [None] * 8}

    def sched_xload(lg, i):
        p = i % 2
        if i >= 2:
            S.w("vv", lnst["done"][i - 2])
        elif lnst["prev"] is not None:
            S.w("vv", lnst["prev"])
        S.dma("sync", f"ldx{p}", xs[p][:],
              d_xp[lg * GW + i * 128: lg * GW + (i + 1) * 128, :])

    def sched_ln_tile(lg, i):
        lgp = lg % 2
        p = i % 2
        S.w(f"ldx{p}", 16 * (i // 2 + 1) + 64 * lg)
        if i >= 1:
            S.w("vv", lnst["done"][i - 1])
        elif lnst["prev"] is not None:
            S.w("vv", lnst["prev"])
        for s8 in range(8):
            S.op("vector", lambda q, p=p, s8=s8: q.bn_stats(
                stats[:, 6 * s8:6 * s8 + 6], xs[p][:, 512 * s8:512 * (s8 + 1)]),
                inc="vv")
        S.w("vv")
        S.op("vector", lambda q: q.bn_aggr(mv[:], stats[:]), inc="vv")
        S.w("vv")
        nado = S.op("vector", lambda q: q.tensor_scalar_add(
            sdt[:], mv[:, 1:2], ccol[:, 3:4]), inc="vv")
        S.w("vv", nado)
        nsq = S.op("scalar", lambda q: q.activation(sdt[:], sdt[:], AF.Sqrt), inc="aa")
        S.w("aa", nsq)
        S.op("vector", lambda q: q.reciprocal(rstd[:], sdt[:]), inc="vv")
        S.w("vv")
        S.w(f"stln{p}", 16 * (i // 2) + 64 * lg)
        lnst["done"][i] = S.op("vector", lambda q, p=p: q.tensor_scalar(
            xstg[p][:], xs[p][:], mv[:, 0:1], rstd[:, 0:1], ALU.subtract, ALU.mult),
            inc="vv")
        S.w("vv", lnst["done"][i])
        S.dma("scalar", f"stln{p}",
              d_xn_tm[lgp * GW + i * 128:lgp * GW + (i + 1) * 128, :], xstg[p][:])
        if i + 2 < 8:
            sched_xload(lg, i + 2)
        if i == 7:
            lnst["prev"] = lnst["done"][7]
            lnst["done"] = [None] * 8

    S.new_block()
    S.fence_dma()
    S.dma("sync", "ctab", cosT[:], d_cosT[:])
    S.dma("sync", "ctab", srtT[:], d_srtT[:])
    ctab_all = S.n("ctab")

    for g in range(NG):
        gp = g % 2

        if g == 0:
            sched_xload(0, 0)
            sched_xload(0, 1)
            for i in range(8):
                sched_ln_tile(0, i)

        S.w("stln0", 64 * (g + 1)).w("stln1", 64 * (g + 1))
        S.w("stxnf", S.n("stxnf"))
        if prev_job_end[1] is not None:
            S.w("tensor", prev_job_end[1])
        S.dma_sems.add("ldT")
        for cchunk in range(NC):
            S.op("sync", lambda q, c=cchunk, gp=gp: q.dma_start_transpose(
                xnT[c][:], d_xn_tm[gp * GW:(gp + 1) * GW, c * 128:(c + 1) * 128]),
                inc="ldT", k=16)
        ldT_all = S.n("ldT")

        S.w("ldT", ldT_all)
        for cchunk in range(NC):
            S.dma("scalar", "stxnf", d_xnf[cchunk, :, g * QW:(g + 1) * QW],
                  xnT[cchunk][:, 0:QW])

        job_end = [None] * NJ
        epi_done = [None] * NJ
        wcast = [None] * NJ
        st_cnt = [S.n("stkq0"), S.n("stkq1")]

        def sched_wdma(j, g=g):
            par = j % 2
            if g == 0:
                if j >= 2 and wcast[j - 2] is not None:
                    S.w("aa", wcast[j - 2])
                S.dma("sync", f"wl32a{j % 2}", wb32[j % 2][0:64, :], wsrc(OORDER[j])[0:64])
                S.dma("scalar", f"wl32b{j % 2}", wb32[j % 2][64:128, :], wsrc(OORDER[j])[64:128])
                S.w(f"wl32a{j % 2}", 16 * (j // 2 + 1)).w(f"wl32b{j % 2}", 16 * (j // 2 + 1))
                if j >= 2:
                    S.w("tensor", job_end[j - 2])
                    S.w(f"wbka{par}", 16 * (j // 2)).w(f"wbkb{par}", 16 * (j // 2))
                elif prev_job_end[par] is not None:
                    S.w("tensor", prev_job_end[par])
                S.op("scalar", lambda q, par=par, j=j: q.copy(wb[par][:], wb32[j % 2][:]), inc="aa")
                wcast[j] = S.n("aa")
            else:
                if j >= 2:
                    S.w("tensor", job_end[j - 2])
                elif prev_job_end[par] is not None:
                    S.w("tensor", prev_job_end[par])
                if j < 2:
                    S.w("wbka0", S.n("wbka0")).w("wbka1", S.n("wbka1"))
                    S.w("wbkb0", S.n("wbkb0")).w("wbkb1", S.n("wbkb1"))
                S.dma("sync", f"wlds{par}", wb[par][:], d_wbf[OORDER[j], :, :])

        sched_wdma(0)
        sched_wdma(1)
        for j in range(NJ):
            par = j % 2
            o = OORDER[j]
            isq = o < 32
            isk = 32 <= o < 64
            W = QW if isq else GW
            adapters = (g == 0 and o >= 32)

            if g == 0:
                S.w("aa", wcast[j])
            else:
                S.w(f"wlds{par}", 16 * (48 * (g - 1) + j // 2 + 1))
            S.w("ldT", ldT_all)
            if j >= 2:
                S.w("vv", epi_done[j - 2])
            elif prev_epi[par] is not None:
                S.w("vv", prev_epi[par])

            def jobfn(q, par=par, W=W, adapters=adapters):
                last = None
                for cc in range(NC):
                    for hf in range(W // 512):
                        last = q.matmul(ps[par][:, hf * 512:(hf + 1) * 512],
                                        wb[par][:, cc * 128:(cc + 1) * 128],
                                        xnT[cc][:, hf * 512:(hf + 1) * 512],
                                        start=(cc == 0), stop=(cc == NC - 1))
                    if adapters:
                        last = q.matmul(akp[par][:],
                                        wb[par][:, cc * 128:(cc + 1) * 128],
                                        awte[:, cc * 10:(cc + 1) * 10],
                                        start=(cc == 0), stop=(cc == NC - 1))
                return last
            job_end[j] = S.op("tensor", jobfn, inc="tensor")

            tc = g * GW
            S.w("tensor", job_end[j])
            S.w(f"stkq{par}", st_cnt[par])
            if j >= 1:
                S.w("vv", epi_done[j - 1])
            elif prev_epi[1] is not None:
                S.w("vv", prev_epi[1])
            if g == 0 and j == 0:
                S.w("ctab", ctab_all)
            if isq or isk:
                S.op("vector", lambda q, par=par, W=W, tc=tc: q.tensor_mul(
                    t1f[:, 0:W], ps[par][:, 0:W], cosT[:, tc:tc + W]), inc="vv")
                S.op("vector", lambda q, par=par, W=W, tc=tc: q.tensor_mul(
                    t2f[0:64, 0:W], ps[par][64:128, 0:W], srtT[64:128, tc:tc + W]), inc="vv")
                S.op("vector", lambda q, par=par, W=W, tc=tc: q.tensor_mul(
                    t2f[64:128, 0:W], ps[par][0:64, 0:W], srtT[0:64, tc:tc + W]), inc="vv")
                S.w("vv")
                epi = S.op("vector", lambda q, par=par, W=W: q.tensor_add(
                    kqs[par][:, 0:W], t1f[:, 0:W], t2f[:, 0:W]), inc="vv")
            else:
                epi = S.op("vector", lambda q, par=par, W=W: q.tensor_copy(
                    kqs[par][:, 0:W], ps[par][:, 0:W]), inc="vv")
            if adapters:
                epi = S.op("vector", lambda q, par=par, o=o: q.tensor_copy(
                    akav[:, (o - 32) * 10:(o - 31) * 10], akp[par][:]), inc="vv")
            epi_done[j] = epi

            S.w("vv", epi_done[j])
            if isq:
                S.dma("scalar", f"stkq{par}", d_q[o, :, g * QW:(g + 1) * QW],
                      kqs[par][:, 0:QW])
            elif isk:
                S.dma("scalar", f"stkq{par}", d_k[o - 32, :, g * GW:(g + 1) * GW],
                      kqs[par][:, 0:GW])
            else:
                S.dma("scalar", f"stkq{par}", d_v[o - 64, :, g * GW:(g + 1) * GW],
                      kqs[par][:, 0:GW])
            st_cnt[par] = S.n(f"stkq{par}")
            if g == 0:
                S.w("tensor", job_end[j])
                S.dma("sync", f"wbka{par}", d_wbf[o, 0:64, :], wb[par][0:64, :])
                S.w("tensor", job_end[j])
                S.dma("scalar", f"wbkb{par}", d_wbf[o, 64:128, :], wb[par][64:128, :])

            if j + 2 < NJ:
                sched_wdma(j + 2)
            if g + 1 < NG:
                if j == 46:
                    sched_xload(g + 1, 0)
                    sched_xload(g + 1, 1)
                elif 48 <= j < 64 and j % 2 == 0:
                    sched_ln_tile(g + 1, (j - 48) // 2)

        prev_job_end = [job_end[NJ - 2], job_end[NJ - 1]]
        prev_epi = [epi_done[NJ - 2], epi_done[NJ - 1]]

    S.emit()

    es1.close()

    es23 = ExitStack()
    yT = [es23.enter_context(nc.sbuf_tensor(f"yT{c}", [128, OWN], BF16)) for c in range(NC)]

    es2 = ExitStack()
    k_sb = es2.enter_context(nc.sbuf_tensor("ksb", [128, TB], BF16))
    vT_sb = es2.enter_context(nc.sbuf_tensor("vTsb", [128, TB], BF16))
    v_sb = [es2.enter_context(nc.sbuf_tensor(f"vsb{i}", [128, TB], BF16)) for i in range(2)]
    q_sb = [es2.enter_context(nc.sbuf_tensor(f"qsb{i}", [128, OWN], BF16)) for i in range(2)]
    p_sb = [es2.enter_context(nc.sbuf_tensor(f"psb{i}", [128, 1024], BF16)) for i in range(4)]
    mq_sb = es2.enter_context(nc.sbuf_tensor("mqsb", [128, 4 * 512], BF16))
    av_sb = [es2.enter_context(nc.sbuf_tensor(f"avsb{i}", [10, 128], BF16)) for i in range(2)]
    pa_sb = es2.enter_context(nc.sbuf_tensor("pasb", [10, 512], BF16))
    pa2_sb = es2.enter_context(nc.sbuf_tensor("pa2sb", [10, 512], BF16))
    bc_sb = es2.enter_context(nc.sbuf_tensor("bcsb", [128, 512], BF16))
    mone = es2.enter_context(nc.sbuf_tensor("mone", [128, 1], F32))
    rs_sb = [es2.enter_context(nc.sbuf_tensor(f"rssb{i}", [1, OWN], F32)) for i in range(2)]
    as_sb = es2.enter_context(nc.sbuf_tensor("assb", [1, 512], F32))

    scp = [es2.enter_context(nc.psum_tensor(f"scp{i}", [128, 1024], F32)) for i in range(2)]
    y_ps = es2.enter_context(nc.psum_tensor("y_ps", [128, 512], F32))
    rs_ps = es2.enter_context(nc.psum_tensor("rs_ps", [1, 512], F32))
    aux_ps = es2.enter_context(nc.psum_tensor("aux_ps", [128, 512], F32))
    vt_ps = es2.enter_context(nc.psum_tensor("vt_ps", [128, 512], BF16))

    S.new_block()
    S.fence_dma()
    S.dma("gpsimd", "mql", mq_sb[:], d_mq.rearrange("m p t -> p m t"))
    for og in range(8):
        S.dma("gpsimd", "wpc", d_wpbf[og],
              d_wpT.rearrange("(c p) o -> p c o", p=128)[:, :, og * 512:(og + 1) * 512])
    for og in range(8):
        S.dma("gpsimd", "wfc", d_wfbf[og],
              d_wfT.rearrange("(c p) o -> p c o", p=128)[:, :, og * 512:(og + 1) * 512])

    S.w("stkq0", S.n("stkq0")).w("stkq1", S.n("stkq1"))
    S.w("stxnf", S.n("stxnf"))
    S.w("stln0", S.n("stln0")).w("stln1", S.n("stln1"))
    S.op("sync", lambda q: q.nop())

    st = {"vtr_pe_last": 0, "vtr_ev": [[], []], "prev_exp": None, "prev_pv": None,
          "prev_yevac": None, "prev_bc": None, "sc_last": {}, "mask_first": True,
          "prev_chain_q": None, "prev_chain_bc": None, "prev_chain_yT": None,
          "prev_chain_done": None, "pidx": 0, "pv_hist": [None, None, None, None],
          "yev": [None, None], "chain_last_vv": [None, None],
          "chain_last_pe": [None, None], "rsinv": [None, None],
          "vtr_prev_ev": None, "vtr_pending": None, "chain_steps": []}

    def load_head_k(h):
        S.dma("sync", "kld", k_sb[:], d_k[h, :, :])

    def load_head_v(h):
        S.w("tensor", st["vtr_pe_last"])
        S.dma("sync", "vTld", vT_sb[:], d_v[h, :, :])

    def vtrans_round(h, r):
        if r < 8:
            S.w("vTld", 16 * (h + 1))
            if st["vtr_prev_ev"] is not None:
                S.w("vv", st["vtr_prev_ev"])

            def tfn(q, r=r):
                last = None
                for b4 in range(4):
                    ell = r * 4 + b4
                    last = q.transpose(vt_ps[:, b4 * 128:(b4 + 1) * 128],
                                       vT_sb[:, ell * 128:(ell + 1) * 128], idb[:])
                return last
            tend = S.op("tensor", tfn, inc="tensor")
            S.w("tensor", tend)
            ev = S.op("vector", lambda q, h=h, r=r: q.tensor_copy(
                v_sb[h % 2][:, r * 512:(r + 1) * 512], vt_ps[:]), inc="vv")
            st["vtr_prev_ev"] = ev
            st["vtr_ev"][h % 2].append(ev)
        else:
            S.w("vv", st["vtr_prev_ev"])
            tav = S.op("tensor", lambda q, h=h: q.transpose(
                vt_ps[0:10, 0:128], akav[:, (32 + h) * 10:(33 + h) * 10], idb[:]),
                inc="tensor")
            S.w("tensor", tav)
            if st["chain_last_pe"][h % 2] is not None:
                S.w("tensor", st["chain_last_pe"][h % 2])
            st["vtr_ev"][h % 2].append(S.op("vector", lambda q, h=h: q.tensor_copy(
                av_sb[h % 2][:, :], vt_ps[0:10, 0:128]), inc="vv"))
            st["vtr_pe_last"] = tav

    def sched_vtrans(h):
        st["vtr_ev"][h % 2] = []
        for r in range(9):
            vtrans_round(h, r)

    load_head_k(0)
    load_head_v(0)
    S.dma("sync", "qld", q_sb[0][:], d_q[0, :, :])
    sched_vtrans(0)
    S.op("vector", lambda q: q.memset(mone[:], -1.0), inc="vv")

    def enqueue_chain(h):
        par = h % 2
        cst = {"last_bc": st["prev_chain_bc"]}

        for g in range(NG):
            def s1(g=g, par=par, h=h):
                S.w("qld", 16 * (h + 1))
                if cst["last_bc"] is not None:
                    S.w("vv", cst["last_bc"])
                cst["as"] = S.op("tensor", lambda q, h=h, g=g, par=par: q.matmul(
                    aux_ps[0:10, :], akav[:, h * 10:(h + 1) * 10],
                    q_sb[par][:, g * 512:(g + 1) * 512], start=True, stop=True),
                    inc="tensor")
                st["prev_chain_q"] = cst["as"]

            def s2():
                S.w("tensor", cst["as"])
                cst["pae"] = S.op("scalar", lambda q: q.activation(
                    pa_sb[:], aux_ps[0:10, :], AF.Exp, scale=ccol[0:10, 2:3]), inc="aa")

            def s3():
                S.w("aa", cst["pae"])
                cst["asum"] = S.op("tensor", lambda q: q.matmul(
                    aux_ps[0:1, :], ones_b[0:10, 0:1], pa_sb[:],
                    start=True, stop=True), inc="tensor")

            def s4():
                S.w("tensor", cst["asum"])
                n_lna = S.op("scalar", lambda q: q.activation(
                    as_sb[:], aux_ps[0:1, :], AF.Ln), inc="aa")
                S.w("aa", n_lna)
                cst["inva"] = S.op("scalar", lambda q: q.activation(
                    as_sb[:], as_sb[:], AF.Exp, scale=mone[0:1, 0:1]), inc="aa")

            def s5():
                S.w("aa", cst["inva"])
                if cst["last_bc"] is not None:
                    S.w("vv", cst["last_bc"])
                cst["f2"] = S.op("vector", lambda q: q.tensor_scalar_mul(
                    as_sb[:], as_sb[:], gat_sb[0:1, 0:1]), inc="vv")

            def s6():
                S.w("vv", cst["f2"])
                cst["bc10"] = S.op("tensor", lambda q: q.matmul(
                    aux_ps[0:10, :], ones_f[0:1, 0:10], as_sb[:],
                    start=True, stop=True), inc="tensor")

            def s7():
                S.w("tensor", cst["bc10"])
                cst["pa2"] = S.op("vector", lambda q: q.tensor_mul(
                    pa2_sb[:], pa_sb[:], aux_ps[0:10, :]), inc="vv")

            def s8(g=g, par=par):
                S.w("vv", cst["pa2"])
                if st["rsinv"][par] is not None and st["rsinv"][par][g] is not None:
                    S.w("aa", st["rsinv"][par][g])
                cst["bc128"] = S.op("tensor", lambda q, par=par, g=g: q.matmul(
                    aux_ps[:], ones_f[0:1, 0:128], rs_sb[par][0:1, g * 512:(g + 1) * 512],
                    start=True, stop=True), inc="tensor")

            def s9():
                S.w("tensor", cst["bc128"])
                cst["bcc"] = S.op("vector", lambda q: q.tensor_copy(bc_sb[:], aux_ps[:]),
                                  inc="vv")

            def s10(g=g, par=par, h=h):
                S.w("vv", cst["bcc"])
                cst["ya"] = S.op("tensor", lambda q, par=par: q.matmul(
                    aux_ps[:], av_sb[par][:, :], pa2_sb[:], start=True, stop=True),
                    inc="tensor")
                S.w("vv", st["yev"][par][g]).w("vv", cst["bcc"])
                S.op("vector", lambda q, h=h, g=g: q.tensor_mul(
                    yT[h][:, g * 512:(g + 1) * 512],
                    yT[h][:, g * 512:(g + 1) * 512], bc_sb[:]), inc="vv")

            def s11(h=h, g=g, par=par):
                S.w("vv").w("tensor", cst["ya"])
                last_yT = S.op("vector", lambda q, h=h, g=g: q.tensor_add(
                    yT[h][:, g * 512:(g + 1) * 512],
                    yT[h][:, g * 512:(g + 1) * 512], aux_ps[:]), inc="vv")
                cst["last_bc"] = last_yT
                st["prev_chain_bc"] = last_yT
                st["chain_last_vv"][par] = last_yT
                st["chain_last_pe"][par] = S.n("tensor")

            st["chain_steps"] += [s1, s2, s3, s4, s5, s6, s7, s8, s9, s10, s11]

    def drain_chain():
        while st["chain_steps"]:
            st["chain_steps"].pop(0)()

    for h in range(H):
        par = h % 2
        qld_target = 16 * (h + 1)
        if h + 1 < H:
            load_head_v(h + 1)
            st["vtr_ev"][(h + 1) % 2] = []
            st["vtr_pending"] = [h + 1, 0]
        if h >= 1:
            enqueue_chain(h - 1)
        yev = [None] * NG
        rsev = [None] * NG
        rsinv = [None] * NG
        par4s = {}
        for g in range(NG):
            P = 4 * g + 4
            exp_done = [None] * P
            pv_done = [None] * P
            sc_done = [None] * P

            def emit_pv(p, P=P, par=par):
                par4 = par4s[p]
                S.w("aa", exp_done[p])
                if p == 0:
                    if st["prev_yevac"] is not None:
                        S.w("vv", st["prev_yevac"])
                    for ev in st["vtr_ev"][par]:
                        S.w("vv", ev)

                def pvfn(q, p=p, par4=par4, P=P, par=par):
                    for b2 in range(2):
                        q.matmul(rs_ps[:], ones_b[:, 0:1],
                                 p_sb[par4][:, b2 * 512:(b2 + 1) * 512],
                                 start=(p == 0 and b2 == 0), stop=(p == P - 1 and b2 == 1))
                    last = None
                    for b2 in range(2):
                        ell = 2 * p + b2
                        last = q.matmul(y_ps[:], v_sb[par][:, ell * 128:(ell + 1) * 128],
                                        p_sb[par4][:, b2 * 512:(b2 + 1) * 512],
                                        start=(p == 0 and b2 == 0), stop=(p == P - 1 and b2 == 1))
                    return last
                pv_done[p] = S.op("tensor", pvfn, inc="tensor")
                st["pv_hist"][par4] = pv_done[p]
                if st["vtr_pending"] is not None:
                    vh, vr = st["vtr_pending"]
                    if vr < 8:
                        vtrans_round(vh, vr)
                        st["vtr_pending"] = [vh, vr + 1]
                if st["chain_steps"]:
                    st["chain_steps"].pop(0)()

            for p in range(P):
                par2 = p % 2
                par4 = (st["pidx"] % 4)
                par4s[p] = par4
                st["pidx"] += 1
                S.w("kld", 16 * (h + 1))
                S.w("qld", qld_target)
                if p >= 2:
                    S.w("aa", exp_done[p - 2])
                elif st["prev_exp"] is not None:
                    S.w("aa", st["prev_exp"])

                def scfn(q, p=p, par2=par2, g=g, par=par):
                    last = None
                    for b2 in range(2):
                        ell = 2 * p + b2
                        last = q.matmul(scp[par2][:, b2 * 512:(b2 + 1) * 512],
                                        k_sb[:, ell * 128:(ell + 1) * 128],
                                        q_sb[par][:, g * 512:(g + 1) * 512],
                                        start=True, stop=True)
                    return last
                sc_done[p] = S.op("tensor", scfn, inc="tensor")

                own_pair = (4 * g <= p < 4 * g + 2)
                partner_pair = (p >= 4 * g + 2)
                mend = None
                if own_pair:
                    s0 = 2 * (p - 4 * g)
                    S.w("tensor", sc_done[p])
                    if st["mask_first"]:
                        S.w("mql", 16)
                        st["mask_first"] = False
                    S.op("vector", lambda q, par2=par2, s0=s0: q.tensor_add(
                        scp[par2][:, 0:512], scp[par2][:, 0:512],
                        mq_sb[:, s0 * 512:(s0 + 1) * 512]), inc="vv")
                    mend = S.op("vector", lambda q, par2=par2, s0=s0: q.tensor_add(
                        scp[par2][:, 512:1024], scp[par2][:, 512:1024],
                        mq_sb[:, (s0 + 1) * 512:(s0 + 2) * 512]), inc="vv")
                if own_pair:
                    S.w("vv", mend)
                else:
                    S.w("tensor", sc_done[p])
                if st["pv_hist"][par4] is not None:
                    S.w("tensor", st["pv_hist"][par4])
                bias_ap = hc_sb[:, 0:1] if partner_pair else ccol[:, 0:1]
                exp_done[p] = S.op("scalar", lambda q, par4=par4, par2=par2, b=bias_ap: q.activation(
                    p_sb[par4][:], scp[par2][:], AF.Exp, bias=b, scale=ccol[:, 1:2]),
                    inc="aa")
                if p >= 2:
                    emit_pv(p - 2)
            emit_pv(P - 2)
            emit_pv(P - 1)

            st["prev_exp"] = exp_done[P - 1]
            st["prev_pv"] = pv_done[P - 1]
            st["sc_last"][h] = sc_done[P - 1]

            S.w("tensor", pv_done[P - 1])
            yev[g] = S.op("vector", lambda q, h=h, g=g: q.tensor_copy(
                yT[h][:, g * 512:(g + 1) * 512], y_ps[:]), inc="vv")
            S.w("vv", yev[g])
            if g == 0 and st["chain_last_pe"][par] is not None:
                S.w("tensor", st["chain_last_pe"][par])
            if g == 0 and st["chain_last_vv"][par] is not None:
                S.w("vv", st["chain_last_vv"][par])
            rsev[g] = S.op("vector", lambda q, par=par, g=g: q.tensor_copy(
                rs_sb[par][0:1, g * 512:(g + 1) * 512], rs_ps[:]), inc="vv")
            st["prev_yevac"] = rsev[g]
        st["yev"][par] = yev
        for g in range(NG):
            S.w("vv", rsev[g])
            S.op("scalar", lambda q, par=par, g=g: q.activation(
                rs_sb[par][0:1, g * 512:(g + 1) * 512],
                rs_sb[par][0:1, g * 512:(g + 1) * 512], AF.Ln), inc="aa")
            S.w("aa")
            rsinv[g] = S.op("scalar", lambda q, par=par, g=g: q.activation(
                rs_sb[par][0:1, g * 512:(g + 1) * 512],
                rs_sb[par][0:1, g * 512:(g + 1) * 512], AF.Exp,
                scale=mone[0:1, 0:1]), inc="aa")
        st["rsinv"][par] = rsinv

        drain_chain()

        if h + 1 < H:
            S.w("tensor", st["sc_last"][h])
            load_head_k(h + 1)
            S.w("qld", 16 * (h + 1))
            if st.get("prev_chain_q") is not None:
                S.w("tensor", st["prev_chain_q"])
            S.dma("sync", "qld", q_sb[(h + 1) % 2][:], d_q[h + 1, :, :])
            if st["vtr_pending"] is not None:
                vh, vr = st["vtr_pending"]
                for r in range(vr, 9):
                    vtrans_round(vh, r)
                st["vtr_pending"] = None

    enqueue_chain(H - 1)
    drain_chain()

    S.emit()
    es2.close()

    es3a = ExitStack()
    wp_sb = es3a.enter_context(nc.sbuf_tensor("wp_sb", [128, 32 * 512], BF16))
    x3 = [es3a.enter_context(nc.sbuf_tensor(f"x3{i}", [128, 512], F32)) for i in range(4)]
    o3 = [es3a.enter_context(nc.sbuf_tensor(f"o3{i}", [128, 512], F32)) for i in range(4)]
    pp = [es3a.enter_context(nc.psum_tensor(f"pp{i}", [128, 512], F32)) for i in range(4)]

    S.new_block()
    S.fence_dma()

    def wp_src(og, hh):
        return d_wpT.rearrange("(c p) o -> p c o", p=128)[
            :, hh * 16:(hh + 1) * 16, og * 512:(og + 1) * 512]

    NT3 = 16

    def x_rows(tb):
        g = tb // 4
        r0 = g * GW + (tb % 4) * 128
        return d_xp[r0:r0 + 128, :]

    tb_end = {}
    ep_end = {}
    stc = {}
    S.w("wpc", S.n("wpc"))
    S.dma("sync", "wp0", wp_sb[:, 0:16 * 512], d_wpbf[0][:, 0:16 * 512])
    S.dma("sync", "wp1", wp_sb[:, 16 * 512:], d_wpbf[0][:, 16 * 512:])
    for tb in range(4):
        S.dma("sync", f"x3l{tb}", x3[tb][:], x_rows(tb)[:, 0:512])
    xcnt = [16, 16, 16, 16]

    for og in range(8):
        for tb in range(NT3):
            slot = tb % 4
            idx = og * NT3 + tb
            S.w("wp0", 16 * (og + 1))
            if idx >= 4:
                S.w("vv", ep_end[idx - 4])

            def fn_a(q, slot=slot, tb=tb):
                last = None
                for cc in range(16):
                    last = q.matmul(pp[slot][:], yT[cc][:, tb * 128:(tb + 1) * 128],
                                    wp_sb[:, cc * 512:(cc + 1) * 512],
                                    start=(cc == 0), stop=False)
                return last
            S.op("tensor", fn_a, inc="tensor")
            S.w("wp1", 16 * (og + 1))

            def fn_b(q, slot=slot, tb=tb):
                last = None
                for cc in range(16, NC):
                    last = q.matmul(pp[slot][:], yT[cc][:, tb * 128:(tb + 1) * 128],
                                    wp_sb[:, cc * 512:(cc + 1) * 512],
                                    start=False, stop=(cc == NC - 1))
                return last
            tb_end[idx] = S.op("tensor", fn_b, inc="tensor")

            S.w("tensor", tb_end[idx])
            S.w(f"x3l{slot}", xcnt[slot])
            S.w(f"st3{slot}", stc.get(idx - 4, 0))
            ep_end[idx] = S.op("vector", lambda q, slot=slot: q.tensor_add(
                o3[slot][:], pp[slot][:], x3[slot][:]), inc="vv")
            S.w("vv", ep_end[idx])
            S.dma("scalar", f"st3{slot}",
                  d_acc[tb * 128:(tb + 1) * 128, og * 512:(og + 1) * 512], o3[slot][:])
            stc[idx] = S.n(f"st3{slot}")

            nidx = idx + 4
            if nidx < 8 * NT3:
                nog, ntb = nidx // NT3, nidx % NT3
                S.w("vv", ep_end[idx])
                S.dma("sync", f"x3l{slot}", x3[slot][:],
                      x_rows(ntb)[:, nog * 512:(nog + 1) * 512])
                xcnt[slot] += 16

        if og + 1 < 8:
            S.w("tensor", tb_end[og * NT3 + NT3 - 1])
            S.dma("sync", "wp0", wp_sb[:, 0:16 * 512], d_wpbf[og + 1][:, 0:16 * 512])
            S.dma("sync", "wp1", wp_sb[:, 16 * 512:], d_wpbf[og + 1][:, 16 * 512:])

    S.emit()
    es3a.close()
    es23.close()

    es3b = ExitStack()
    xn3 = [es3b.enter_context(nc.sbuf_tensor(f"xn3{c}", [128, OWN], BF16))
           for c in range(NC)]
    wf_sb = es3b.enter_context(nc.sbuf_tensor("wf_sb", [128, 32 * 512], BF16))
    bt_sb = es3b.enter_context(nc.sbuf_tensor("btsb", [64, C], BF16))
    at_sb = es3b.enter_context(nc.sbuf_tensor("atsb", [128, NC * 64], BF16))
    gw_sb = es3b.enter_context(nc.sbuf_tensor("gwsb", [128, NC * 4], BF16))
    rkm_sb = es3b.enter_context(nc.sbuf_tensor("rkmsb", [4, 64], BF16))
    gbbc_sb = es3b.enter_context(nc.sbuf_tensor("gbbcsb", [128, 4], F32))
    epsio_sb = es3b.enter_context(nc.sbuf_tensor("epsiosb", [128, 4], F32))
    bbc_sb = [es3b.enter_context(nc.sbuf_tensor(f"bbcsb{i}", [128, 512], BF16))
              for i in range(2)]
    gs_sb = es3b.enter_context(nc.sbuf_tensor("gssb", [128, 4], F32))
    gm_sb = es3b.enter_context(nc.sbuf_tensor("gmsb", [128, 1], F32))
    sel_sb = es3b.enter_context(nc.sbuf_tensor("selsb", [128, 4], BF16))
    selT_sb = es3b.enter_context(nc.sbuf_tensor("selTsb", [4, OWN], BF16))
    u_sb = es3b.enter_context(nc.sbuf_tensor("usb", [64, 512], F32))
    mu_sb = es3b.enter_context(nc.sbuf_tensor("musb", [64, OWN], BF16))
    acc3 = [es3b.enter_context(nc.sbuf_tensor(f"acc3{i}", [128, 512], F32))
            for i in range(2)]
    ob3 = [es3b.enter_context(nc.sbuf_tensor(f"ob3{i}", [128, 512], F32))
           for i in range(2)]
    gps = es3b.enter_context(nc.psum_tensor("gps", [128, 4], F32))
    stp = es3b.enter_context(nc.psum_tensor("stp", [4, 128], BF16))
    ups = es3b.enter_context(nc.psum_tensor("ups", [64, 512], F32))
    smp = es3b.enter_context(nc.psum_tensor("smp", [64, 512], F32))
    pf = [es3b.enter_context(nc.psum_tensor(f"pf{i}", [128, 512], F32)) for i in range(4)]

    S.new_block()
    S.fence_dma()
    S.dma("gpsimd", "ld3b", bt_sb[:], d_BT[:])
    S.dma("gpsimd", "ld3b", at_sb[:], d_AT.rearrange("(c p) r -> p c r", p=128))
    S.dma("gpsimd", "ld3b", gw_sb[:], d_gwT.rearrange("(c p) e -> p c e", p=128))
    S.dma("gpsimd", "ld3b", rkm_sb[:], d_rkm[:])

    S.dma("sync", "ld3s", gbbc_sb[:], d_gbbc[:])
    S.dma("sync", "ld3s", epsio_sb[:], d_epsio[:])
    S.w("stxnf", S.n("stxnf"))
    for cchunk in range(NC):
        S.dma("sync", "xn3l", xn3[cchunk][:], d_xnf[cchunk, :, :])
    ld3b_all = S.n("ld3b")
    ld3s_all = S.n("ld3s")
    xn3_all = S.n("xn3l")

    n_sel = None
    n_selT = None
    n_mu = None
    for Q in range(4):
        for tbq in range(4):
            tb = Q * 4 + tbq
            S.w("ld3b", ld3b_all).w("xn3l", xn3_all)
            if n_sel is not None:
                S.w("vv", n_sel)

            def gfn(q, tb=tb):
                last = None
                for cc in range(NC):
                    last = q.matmul(gps[:], xn3[cc][:, tb * 128:(tb + 1) * 128],
                                    gw_sb[:, cc * 4:(cc + 1) * 4],
                                    start=(cc == 0), stop=(cc == NC - 1))
                return last
            n_g = S.op("tensor", gfn, inc="tensor")
            S.w("tensor", n_g).w("ld3s", ld3s_all)
            S.op("vector", lambda q: q.tensor_add(gs_sb[:], gps[:], gbbc_sb[:]), inc="vv")
            S.w("vv")
            S.op("vector", lambda q: q.tensor_sub(gs_sb[:], gs_sb[:], epsio_sb[:]), inc="vv")
            S.w("vv")
            S.op("vector", lambda q: q.reduce_max(gm_sb[:], gs_sb[:], axis=AX.X), inc="vv")
            S.w("vv")
            if n_selT is not None:
                S.w("tensor", prev_nt)
            n_sel = S.op("vector", lambda q: q.tensor_single_scalar(
                sel_sb[:], gs_sb[:], gm_sb[:, 0:1], ALU.is_equal), inc="vv")
            S.w("vv", n_sel)
            if n_selT is not None:
                S.w("vv", n_selT)
            n_t = S.op("tensor", lambda q: q.transpose(stp[:], sel_sb[:], idb[:]),
                       inc="tensor")
            prev_nt = n_t
            S.w("tensor", n_t)
            n_selT = S.op("vector", lambda q, tb=tb: q.tensor_copy(
                selT_sb[:, tb * 128:(tb + 1) * 128], stp[:]), inc="vv")

        S.w("ld3b", ld3b_all).w("xn3l", xn3_all)
        if n_mu is not None:
            S.w("vv", n_mu)

        def ufn(q, Q=Q):
            last = None
            for cc in range(NC):
                last = q.matmul(ups[:], at_sb[:, cc * 64:(cc + 1) * 64],
                                xn3[cc][:, Q * 512:(Q + 1) * 512],
                                start=(cc == 0), stop=(cc == NC - 1))
            return last
        n_u = S.op("tensor", ufn, inc="tensor")
        S.w("tensor", n_u)
        n_ue = S.op("vector", lambda q: q.tensor_copy(u_sb[:], ups[:]), inc="vv")
        S.w("vv", n_selT)
        if n_mu is not None:
            S.w("vv", n_mu)
        n_sm = S.op("tensor", lambda q, Q=Q: q.matmul(
            smp[:], rkm_sb[:], selT_sb[:, Q * 512:(Q + 1) * 512],
            start=True, stop=True), inc="tensor")
        S.w("tensor", n_sm).w("vv", n_ue)
        n_mu = S.op("vector", lambda q, Q=Q: q.tensor_mul(
            mu_sb[:, Q * 512:(Q + 1) * 512], u_sb[:], smp[:]), inc="vv")

    def wf_src(og, hh):
        return d_wfT.rearrange("(c p) o -> p c o", p=128)[
            :, hh * 16:(hh + 1) * 16, og * 512:(og + 1) * 512]

    S.w("wfc", S.n("wfc"))
    S.dma("sync", "wf0", wf_sb[:, 0:16 * 512], d_wfbf[0][:, 0:16 * 512])
    S.dma("sync", "wf1", wf_sb[:, 16 * 512:], d_wfbf[0][:, 16 * 512:])
    S.dma("gpsimd", "bbcl0", bbc_sb[0][:], d_bbc[:, 0:512])
    S.dma("gpsimd", "bbcl1", bbc_sb[1][:], d_bbc[:, 512:1024])
    for slot in range(4):
        S.w(f"st3{slot}", S.n(f"st3{slot}"))
    for tb in range(2):
        S.dma("sync", f"ac{tb}", acc3[tb][:], d_acc[tb * 128:(tb + 1) * 128, 0:512])
    accnt = [16, 16]
    tb_end = {}
    ep_end = {}
    stc = {}
    for og in range(8):
        for tb in range(NT3):
            slot = tb % 4
            sl2 = tb % 2
            idx = og * NT3 + tb
            S.w("wf0", 16 * (og + 1))
            S.w("vv", n_mu)
            if idx >= 4:
                S.w("vv", ep_end[idx - 4])

            def fn_a(q, slot=slot, tb=tb):
                last = None
                for cc in range(16):
                    last = q.matmul(pf[slot][:], xn3[cc][:, tb * 128:(tb + 1) * 128],
                                    wf_sb[:, cc * 512:(cc + 1) * 512],
                                    start=(cc == 0), stop=False)
                return last
            S.op("tensor", fn_a, inc="tensor")
            S.w("wf1", 16 * (og + 1))

            def fn_b(q, slot=slot, tb=tb, og=og):
                last = None
                for cc in range(16, NC):
                    last = q.matmul(pf[slot][:], xn3[cc][:, tb * 128:(tb + 1) * 128],
                                    wf_sb[:, cc * 512:(cc + 1) * 512],
                                    start=False, stop=False)
                last = q.matmul(pf[slot][:], mu_sb[:, tb * 128:(tb + 1) * 128],
                                bt_sb[:, og * 512:(og + 1) * 512],
                                start=False, stop=True)
                return last
            tb_end[idx] = S.op("tensor", fn_b, inc="tensor")

            S.w("tensor", tb_end[idx])
            S.w(f"ac{sl2}", accnt[sl2])
            S.w(f"bbcl{og % 2}", 16 * (og // 2 + 1))
            if idx == 0:
                S.w("ld3b", ld3b_all)
            if idx >= 2:
                S.w(f"so{sl2}", stc.get(idx - 2, 0))
            S.op("vector", lambda q, slot=slot, sl2=sl2: q.tensor_add(
                ob3[sl2][:], pf[slot][:], acc3[sl2][:]), inc="vv")
            S.w("vv")
            ep_end[idx] = S.op("vector", lambda q, sl2=sl2, og=og: q.tensor_add(
                ob3[sl2][:], ob3[sl2][:], bbc_sb[og % 2][:]), inc="vv")
            S.w("vv", ep_end[idx])
            S.dma("scalar", f"so{sl2}",
                  d_out[tb * 128:(tb + 1) * 128, og * 512:(og + 1) * 512], ob3[sl2][:])
            stc[idx] = S.n(f"so{sl2}")

            nidx = idx + 2
            if nidx < 8 * NT3:
                nog, ntb = nidx // NT3, nidx % NT3
                S.w("vv", ep_end[idx])
                S.dma("sync", f"ac{sl2}", acc3[sl2][:],
                      d_acc[ntb * 128:(ntb + 1) * 128, nog * 512:(nog + 1) * 512])
                accnt[sl2] += 16

        if og + 1 < 8:
            S.w("tensor", tb_end[og * NT3 + NT3 - 1])
            S.dma("sync", "wf0", wf_sb[:, 0:16 * 512], d_wfbf[og + 1][:, 0:16 * 512])
            S.dma("sync", "wf1", wf_sb[:, 16 * 512:], d_wfbf[og + 1][:, 16 * 512:])
            if og + 2 < 8:
                S.w("vv", ep_end[og * NT3 + NT3 - 1])
                S.dma("gpsimd", f"bbcl{og % 2}", bbc_sb[og % 2][:],
                      d_bbc[:, (og + 2) * 512:(og + 3) * 512])

    for slot in range(2):
        S.w(f"so{slot}", S.n(f"so{slot}"))
    S.op("scalar", lambda q: q.nop())
    S.emit()
    es3b.close()
    es0.close()
    return nc



def _tables(h2):
    pos = np.empty(TB, np.int64)
    for g in range(NG):
        own = 2 * g + h2
        other = 2 * g + 1 - h2
        pos[g * GW:g * GW + 512] = np.arange(own * 512, own * 512 + 512)
        pos[g * GW + 512:(g + 1) * GW] = np.arange(other * 512, other * 512 + 512)
    theta = 1.0 / (10000.0 ** (np.arange(0, 128, 2, dtype=np.float64) / 128.0))
    ang = pos[None, :] * theta[:, None]
    cos = np.cos(ang).astype(np.float32)
    sin = np.sin(ang).astype(np.float32)
    cosT = np.concatenate([cos, cos], 0)
    srtT = np.concatenate([sin, -sin], 0)
    return (np.ascontiguousarray(cosT).astype(ml_dtypes.bfloat16),
            np.ascontiguousarray(srtT).astype(ml_dtypes.bfloat16))


def _maskq():
    m = np.zeros((4, 128, 512), np.float32)
    kk, rr = np.meshgrid(np.arange(128), np.arange(128), indexing="ij")
    tri = np.where(kk <= rr, 0.0, NEG).astype(np.float32)
    for s in range(4):
        for j in range(4):
            if j < s:
                m[s, :, j * 128:(j + 1) * 128] = NEG
            elif j == s:
                m[s, :, j * 128:(j + 1) * 128] = tri
    return m


def kernel(L0_attn_x, L0_attn_w_attn, L0_attn_w_proj, L0_attn_adapter_wte,
           L0_attn_gating, L0_ffn_W, L0_ffn_b, L0_ffn_A, L0_ffn_B,
           L0_ffn_gate_w, L0_ffn_gate_b):
    x = np.asarray(L0_attn_x)
    wqkT = np.ascontiguousarray(np.asarray(L0_attn_w_attn).T)
    wpT = np.ascontiguousarray(np.asarray(L0_attn_w_proj).T)
    wfT = np.ascontiguousarray(np.asarray(L0_ffn_W).T)
    awT = np.ascontiguousarray(np.asarray(L0_attn_adapter_wte).T)
    AT = np.ascontiguousarray(np.asarray(L0_ffn_A).T)
    BT = np.ascontiguousarray(np.asarray(L0_ffn_B).T)
    gwT = np.ascontiguousarray(np.asarray(L0_ffn_gate_w).T)
    gating = np.ascontiguousarray(np.asarray(L0_attn_gating).reshape(1, 1).astype(np.float32))
    bbc = np.ascontiguousarray(np.broadcast_to(
        np.asarray(L0_ffn_b).astype(np.float32)[None, :], (128, C)))
    gbbc = np.ascontiguousarray(np.broadcast_to(
        np.asarray(L0_ffn_gate_b).astype(np.float32)[None, :], (128, 4)))
    epsio = np.ascontiguousarray(np.broadcast_to(
        (np.arange(4, dtype=np.float32) * 1e-6)[None, :], (128, 4)))
    rkm = np.zeros((4, 64), np.float32)
    for e, r in enumerate((8, 16, 32, 64)):
        rkm[e, :r] = 1.0
    mq = _maskq()
    ident = np.eye(128, dtype=np.float32)

    in_maps = []
    for core in range(8):
        b, h2 = core // 2, core % 2
        xb = np.asarray(x[b])
        xp = np.empty((TB, C), np.float32)
        for g in range(NG):
            own = 2 * g + h2
            other = 2 * g + 1 - h2
            xp[g * GW:g * GW + 512] = xb[own * 512:(own + 1) * 512]
            xp[g * GW + 512:(g + 1) * GW] = xb[other * 512:(other + 1) * 512]
        cosT, srtT = _tables(h2)
        hcm = np.zeros((128, 2), np.float32)
        hcm[:, 0] = EXPB + (NEG * S128 if h2 == 0 else 0.0)
        in_maps.append({
            "xp": xp, "wqkT": wqkT, "wpT": wpT, "wfT": wfT, "awT": awT,
            "AT": AT, "BTm": BT, "gwT": gwT, "cosT": cosT, "srtT": srtT,
            "mq": mq, "hc": hcm, "gating": gating, "gbbc": gbbc,
            "epsio": epsio, "rkm": rkm, "bbc": bbc, "identf": ident,
        })

    nc = build_program()
    res = run_bass_kernel_spmd(nc, in_maps, core_ids=list(range(8)))

    out = np.empty((4, TB, C), np.float32)
    for core in range(8):
        b, h2 = core // 2, core % 2
        oc = res.results[core]["out"]
        for g in range(NG):
            own = 2 * g + h2
            out[b, own * 512:(own + 1) * 512] = oc[g * 512:(g + 1) * 512]
    return out


# version 19
# speedup vs baseline: 1.1951x; 1.1951x over previous
import numpy as np
import ml_dtypes
from contextlib import ExitStack

import concourse.bass as bass
import concourse.mybir as mybir
from concourse.bass_utils import run_bass_kernel_spmd

F32 = mybir.dt.float32
BF16 = mybir.dt.bfloat16
AF = mybir.ActivationFunctionType
ALU = mybir.AluOpType
AX = mybir.AxisListType

C = 4096
NC = 32
H = 32
TB = 4096
NG = 4
GW = 1024
OWN = 2048
QW = 512
NEG = -32768.0
EXPB = -20.0
S128 = 1.0 / float(np.sqrt(128.0))
S64 = 1.0 / 64.0

ENGS = ("sync", "gpsimd", "tensor", "vector", "scalar")


class Sched:

    def __init__(self, nc, es):
        self.nc = nc
        self.sems = {}
        self.es = es
        self.dma_sems = set()

    def sem(self, name):
        if name not in self.sems:
            self.sems[name] = [self.es.enter_context(self.nc.semaphore(name)), 0]
        return self.sems[name]

    def n(self, name):
        return self.sem(name)[1]

    def new_block(self):
        self.ops = {e: [] for e in ENGS}
        self._w = []

    def w(self, name, val=None):
        s = self.sem(name)
        self._w.append((s[0], s[1] if val is None else val))
        return self

    def op(self, eng, fn, inc=None, k=1):
        waits = self._w
        self._w = []
        tgt = None
        if inc is not None:
            s = self.sem(inc)
            s[1] += k
            tgt = (s[0], k)
        self.ops[eng].append((waits, fn, tgt))
        return self.n(inc) if inc else None

    def dma(self, eng, inc, out, in_):
        self.dma_sems.add(inc)
        return self.op(eng, lambda q, o=out, i=in_: q.dma_start(out=o, in_=i),
                       inc=inc, k=16)

    def fence_dma(self):
        for e in ENGS:
            for name in sorted(self.dma_sems):
                self.w(name, self.n(name))
            for name in ("aa", "vv", "tensor"):
                if name in self.sems:
                    self.w(name, self.n(name))
            self.op(e, lambda q: q.nop())

    def emit(self):
        with self.nc.Block() as block:
            for e in ENGS:
                if not self.ops[e]:
                    continue
                sec = getattr(block, e)

                def emit_section(q, _ops=self.ops[e]):
                    for waits, fn, tgt in _ops:
                        for (sh, v) in waits:
                            q.wait_ge(sh, v)
                        inst = fn(q)
                        if tgt is not None:
                            inst.then_inc(tgt[0], tgt[1])
                sec(emit_section)


def build_program():
    nc = bass.Bass("TRN2", target_bir_lowering=False)

    d_xp = nc.dram_tensor("xp", [TB, C], F32, kind="ExternalInput").ap()
    d_wqkT = nc.dram_tensor("wqkT", [C, 3 * C], F32, kind="ExternalInput").ap()
    d_wpT = nc.dram_tensor("wpT", [C, C], F32, kind="ExternalInput").ap()
    d_wfT = nc.dram_tensor("wfT", [C, C], F32, kind="ExternalInput").ap()
    d_awT = nc.dram_tensor("awT", [C, 10], F32, kind="ExternalInput").ap()
    d_AT = nc.dram_tensor("AT", [C, 64], F32, kind="ExternalInput").ap()
    d_BT = nc.dram_tensor("BTm", [64, C], F32, kind="ExternalInput").ap()
    d_gwT = nc.dram_tensor("gwT", [C, 4], F32, kind="ExternalInput").ap()
    d_cosT = nc.dram_tensor("cosT", [128, TB], BF16, kind="ExternalInput").ap()
    d_srtT = nc.dram_tensor("srtT", [128, TB], BF16, kind="ExternalInput").ap()
    d_mq = nc.dram_tensor("mq", [4, 128, 512], F32, kind="ExternalInput").ap()
    d_hc = nc.dram_tensor("hc", [128, 2], F32, kind="ExternalInput").ap()
    d_gating = nc.dram_tensor("gating", [1, 1], F32, kind="ExternalInput").ap()
    d_gbbc = nc.dram_tensor("gbbc", [128, 4], F32, kind="ExternalInput").ap()
    d_epsio = nc.dram_tensor("epsio", [128, 4], F32, kind="ExternalInput").ap()
    d_rkm = nc.dram_tensor("rkm", [4, 64], F32, kind="ExternalInput").ap()
    d_bbc = nc.dram_tensor("bbc", [128, C], F32, kind="ExternalInput").ap()
    d_ident = nc.dram_tensor("identf", [128, 128], F32, kind="ExternalInput").ap()

    d_out = nc.dram_tensor("out", [OWN, C], F32, kind="ExternalOutput").ap()
    d_xn_tm = nc.dram_tensor("xn_tm", [2 * GW, C], BF16, kind="Internal").ap()
    d_xnf = nc.dram_tensor("xnf", [NC, 128, OWN], BF16, kind="Internal").ap()
    d_q = nc.dram_tensor("q_scr", [H, 128, OWN], BF16, kind="Internal").ap()
    d_k = nc.dram_tensor("k_scr", [H, 128, TB], BF16, kind="Internal").ap()
    d_v = nc.dram_tensor("v_scr", [H, 128, TB], BF16, kind="Internal").ap()
    d_acc = nc.dram_tensor("acc_scr", [OWN, C], F32, kind="Internal").ap()
    d_wbf = nc.dram_tensor("wbf_scr", [96, 128, C], BF16, kind="Internal").ap()
    d_wpbf = nc.dram_tensor("wpbf_scr", [8, 128, 32 * 512], BF16, kind="Internal").ap()
    d_wfbf = nc.dram_tensor("wfbf_scr", [8, 128, 32 * 512], BF16, kind="Internal").ap()

    es0 = ExitStack()
    S = Sched(nc, es0)

    ccol = es0.enter_context(nc.sbuf_tensor("ccol", [128, 8], F32))
    hc_sb = es0.enter_context(nc.sbuf_tensor("hc_sb", [128, 2], F32))
    gat_sb = es0.enter_context(nc.sbuf_tensor("gat_sb", [1, 1], F32))
    ones_b = es0.enter_context(nc.sbuf_tensor("ones_b", [128, 128], BF16))
    ones_f = es0.enter_context(nc.sbuf_tensor("ones_f", [1, 128], F32))
    idb = es0.enter_context(nc.sbuf_tensor("idb", [128, 128], BF16))
    akav = es0.enter_context(nc.sbuf_tensor("akav", [128, 64 * 10], BF16))
    awte = es0.enter_context(nc.sbuf_tensor("awte", [128, NC * 10], BF16))

    S.new_block()
    S.dma("sync", "c0", hc_sb[:], d_hc[:])
    S.dma("sync", "c0", gat_sb[:], d_gating[:])
    S.dma("gpsimd", "g0", idb[:], d_ident[:])
    S.dma("gpsimd", "g0", awte[:], d_awT.rearrange("(c p) j -> p c j", p=128))
    S.op("vector", lambda q: q.memset(ccol[:, 0:1], EXPB), inc="vv")
    S.op("vector", lambda q: q.memset(ccol[:, 1:2], S128), inc="vv")
    S.op("vector", lambda q: q.memset(ccol[:, 2:3], S64), inc="vv")
    S.op("vector", lambda q: q.memset(ccol[:, 3:4], 1e-5), inc="vv")
    S.op("vector", lambda q: q.memset(ones_b[:], 1.0), inc="vv")
    S.op("vector", lambda q: q.memset(ones_f[:], 1.0), inc="vv")
    S.w("c0", S.n("c0")).w("g0", S.n("g0"))
    S.op("sync", lambda q: q.nop())
    S.emit()

    es1 = ExitStack()
    xs = [es1.enter_context(nc.sbuf_tensor(f"xs{i}", [128, C], F32)) for i in range(2)]
    stats = es1.enter_context(nc.sbuf_tensor("stats", [128, 48], F32))
    mv = es1.enter_context(nc.sbuf_tensor("mv", [128, 2], F32))
    sdt = es1.enter_context(nc.sbuf_tensor("sdt", [128, 1], F32))
    rstd = es1.enter_context(nc.sbuf_tensor("rstd", [128, 1], F32))
    xstg = [es1.enter_context(nc.sbuf_tensor(f"xstg{i}", [128, C], BF16)) for i in range(2)]
    xnT = [es1.enter_context(nc.sbuf_tensor(f"xnT{c}", [128, GW], BF16)) for c in range(NC)]
    cosT = es1.enter_context(nc.sbuf_tensor("cosT_sb", [128, TB], BF16))
    srtT = es1.enter_context(nc.sbuf_tensor("srtT_sb", [128, TB], BF16))
    wb = [es1.enter_context(nc.sbuf_tensor(f"wb{i}", [128, C], BF16)) for i in range(2)]
    wb32 = [es1.enter_context(nc.sbuf_tensor(f"wb32_{i}", [128, C], F32)) for i in range(2)]
    kqs = [es1.enter_context(nc.sbuf_tensor(f"kqs{i}", [128, GW], BF16)) for i in range(2)]
    t1f = es1.enter_context(nc.sbuf_tensor("t1f", [128, GW], F32))
    t2f = es1.enter_context(nc.sbuf_tensor("t2f", [128, GW], F32))
    ps = [es1.enter_context(nc.psum_tensor(f"ps{i}", [128, GW], F32)) for i in range(2)]
    akp = [es1.enter_context(nc.psum_tensor(f"akp{i}", [128, 10], F32)) for i in range(2)]

    NJ = 96
    OORDER = list(range(32, 96)) + list(range(32))

    def wsrc(o):
        return d_wqkT.rearrange("(c p) o -> p c o", p=128)[:, :, o * 128:(o + 1) * 128]

    prev_job_end = [None, None]
    prev_epi = [None, None]
    lnst = {"prev": None, "done": [None] * 8}

    def sched_xload(lg, i):
        p = i % 2
        if i >= 2:
            S.w("vv", lnst["done"][i - 2])
        elif lnst["prev"] is not None:
            S.w("vv", lnst["prev"])
        S.dma("sync", f"ldx{p}", xs[p][:],
              d_xp[lg * GW + i * 128: lg * GW + (i + 1) * 128, :])

    def sched_ln_tile(lg, i):
        lgp = lg % 2
        p = i % 2
        S.w(f"ldx{p}", 16 * (i // 2 + 1) + 64 * lg)
        if i >= 1:
            S.w("vv", lnst["done"][i - 1])
        elif lnst["prev"] is not None:
            S.w("vv", lnst["prev"])
        for s8 in range(8):
            S.op("vector", lambda q, p=p, s8=s8: q.bn_stats(
                stats[:, 6 * s8:6 * s8 + 6], xs[p][:, 512 * s8:512 * (s8 + 1)]),
                inc="vv")
        S.w("vv")
        S.op("vector", lambda q: q.bn_aggr(mv[:], stats[:]), inc="vv")
        S.w("vv")
        nado = S.op("vector", lambda q: q.tensor_scalar_add(
            sdt[:], mv[:, 1:2], ccol[:, 3:4]), inc="vv")
        S.w("vv", nado)
        nsq = S.op("scalar", lambda q: q.activation(sdt[:], sdt[:], AF.Sqrt), inc="aa")
        S.w("aa", nsq)
        S.op("vector", lambda q: q.reciprocal(rstd[:], sdt[:]), inc="vv")
        S.w("vv")
        S.w(f"stln{p}", 16 * (i // 2) + 64 * lg)
        lnst["done"][i] = S.op("vector", lambda q, p=p: q.tensor_scalar(
            xstg[p][:], xs[p][:], mv[:, 0:1], rstd[:, 0:1], ALU.subtract, ALU.mult),
            inc="vv")
        S.w("vv", lnst["done"][i])
        S.dma("scalar", f"stln{p}",
              d_xn_tm[lgp * GW + i * 128:lgp * GW + (i + 1) * 128, :], xstg[p][:])
        if i + 2 < 8:
            sched_xload(lg, i + 2)
        if i == 7:
            lnst["prev"] = lnst["done"][7]
            lnst["done"] = [None] * 8

    S.new_block()
    S.fence_dma()
    S.dma("sync", "ctab", cosT[:], d_cosT[:])
    S.dma("sync", "ctab", srtT[:], d_srtT[:])
    ctab_all = S.n("ctab")

    for g in range(NG):
        gp = g % 2

        if g == 0:
            sched_xload(0, 0)
            sched_xload(0, 1)
            for i in range(8):
                sched_ln_tile(0, i)

        S.w("stln0", 64 * (g + 1)).w("stln1", 64 * (g + 1))
        S.w("stxnf", S.n("stxnf"))
        if prev_job_end[1] is not None:
            S.w("tensor", prev_job_end[1])
        S.dma_sems.add("ldT")
        for cchunk in range(NC):
            S.op("sync", lambda q, c=cchunk, gp=gp: q.dma_start_transpose(
                xnT[c][:], d_xn_tm[gp * GW:(gp + 1) * GW, c * 128:(c + 1) * 128]),
                inc="ldT", k=16)
        ldT_all = S.n("ldT")

        S.w("ldT", ldT_all)
        for cchunk in range(NC):
            S.dma("scalar", "stxnf", d_xnf[cchunk, :, g * QW:(g + 1) * QW],
                  xnT[cchunk][:, 0:QW])

        job_end = [None] * NJ
        epi_done = [None] * NJ
        wcast = [None] * NJ
        st_cnt = [S.n("stkq0"), S.n("stkq1")]

        def sched_wload(j):
            if j >= 2 and wcast[j - 2] is not None:
                S.w("aa", wcast[j - 2])
            S.dma("sync", f"wl32a{j % 2}", wb32[j % 2][0:64, :], wsrc(OORDER[j])[0:64])
            S.dma("scalar", f"wl32b{j % 2}", wb32[j % 2][64:128, :], wsrc(OORDER[j])[64:128])

        def sched_wcast(j):
            par = j % 2
            S.w(f"wl32a{j % 2}", 16 * (j // 2 + 1)).w(f"wl32b{j % 2}", 16 * (j // 2 + 1))
            if j >= 2:
                S.w("tensor", job_end[j - 2])
                S.w(f"stwbg{par}", 16 * (j // 2))
            elif prev_job_end[par] is not None:
                S.w("tensor", prev_job_end[par])
            S.op("scalar", lambda q, par=par, j=j: q.copy(wb[par][:], wb32[j % 2][:]), inc="aa")
            wcast[j] = S.n("aa")

        def sched_wdma(j, g=g):
            par = j % 2
            if g == 0:
                sched_wload(j)
                sched_wcast(j)
            else:
                if j >= 2:
                    S.w("tensor", job_end[j - 2])
                elif prev_job_end[par] is not None:
                    S.w("tensor", prev_job_end[par])
                if j < 2:
                    S.w("stwbg0", S.n("stwbg0")).w("stwbg1", S.n("stwbg1"))
                S.dma("sync", f"wlds{par}", wb[par][:], d_wbf[OORDER[j], :, :])

        sched_wdma(0)
        sched_wdma(1)
        for j in range(NJ):
            par = j % 2
            o = OORDER[j]
            isq = o < 32
            isk = 32 <= o < 64
            W = QW if isq else GW
            adapters = (g == 0 and o >= 32)

            if g == 0:
                if j + 2 < NJ:
                    sched_wload(j + 2)
                S.w("aa", wcast[j])
            else:
                S.w(f"wlds{par}", 16 * (48 * (g - 1) + j // 2 + 1))
            S.w("ldT", ldT_all)
            if j >= 2:
                S.w("vv", epi_done[j - 2])
            elif prev_epi[par] is not None:
                S.w("vv", prev_epi[par])

            def jobfn(q, par=par, W=W, adapters=adapters):
                last = None
                for cc in range(NC):
                    for hf in range(W // 512):
                        last = q.matmul(ps[par][:, hf * 512:(hf + 1) * 512],
                                        wb[par][:, cc * 128:(cc + 1) * 128],
                                        xnT[cc][:, hf * 512:(hf + 1) * 512],
                                        start=(cc == 0), stop=(cc == NC - 1))
                    if adapters:
                        last = q.matmul(akp[par][:],
                                        wb[par][:, cc * 128:(cc + 1) * 128],
                                        awte[:, cc * 10:(cc + 1) * 10],
                                        start=(cc == 0), stop=(cc == NC - 1))
                return last
            job_end[j] = S.op("tensor", jobfn, inc="tensor")

            tc = g * GW
            S.w("tensor", job_end[j])
            S.w(f"stkq{par}", st_cnt[par])
            if j >= 1:
                S.w("vv", epi_done[j - 1])
            elif prev_epi[1] is not None:
                S.w("vv", prev_epi[1])
            if g == 0 and j == 0:
                S.w("ctab", ctab_all)
            if isq or isk:
                S.op("vector", lambda q, par=par, W=W, tc=tc: q.tensor_mul(
                    t1f[:, 0:W], ps[par][:, 0:W], cosT[:, tc:tc + W]), inc="vv")
                S.op("vector", lambda q, par=par, W=W, tc=tc: q.tensor_mul(
                    t2f[0:64, 0:W], ps[par][64:128, 0:W], srtT[64:128, tc:tc + W]), inc="vv")
                S.op("vector", lambda q, par=par, W=W, tc=tc: q.tensor_mul(
                    t2f[64:128, 0:W], ps[par][0:64, 0:W], srtT[0:64, tc:tc + W]), inc="vv")
                S.w("vv")
                epi = S.op("vector", lambda q, par=par, W=W: q.tensor_add(
                    kqs[par][:, 0:W], t1f[:, 0:W], t2f[:, 0:W]), inc="vv")
            else:
                epi = S.op("vector", lambda q, par=par, W=W: q.tensor_copy(
                    kqs[par][:, 0:W], ps[par][:, 0:W]), inc="vv")
            if adapters:
                epi = S.op("vector", lambda q, par=par, o=o: q.tensor_copy(
                    akav[:, (o - 32) * 10:(o - 31) * 10], akp[par][:]), inc="vv")
            epi_done[j] = epi

            S.w("vv", epi_done[j])
            if isq:
                S.dma("scalar", f"stkq{par}", d_q[o, :, g * QW:(g + 1) * QW],
                      kqs[par][:, 0:QW])
            elif isk:
                S.dma("scalar", f"stkq{par}", d_k[o - 32, :, g * GW:(g + 1) * GW],
                      kqs[par][:, 0:GW])
            else:
                S.dma("scalar", f"stkq{par}", d_v[o - 64, :, g * GW:(g + 1) * GW],
                      kqs[par][:, 0:GW])
            st_cnt[par] = S.n(f"stkq{par}")
            if g == 0:
                S.w("aa", wcast[j])
                S.dma("scalar", f"stwbg{par}", d_wbf[o, :, :], wb[par][:])

            if j + 2 < NJ:
                if g == 0:
                    sched_wcast(j + 2)
                else:
                    sched_wdma(j + 2)
            if g + 1 < NG:
                if j == 46:
                    sched_xload(g + 1, 0)
                    sched_xload(g + 1, 1)
                elif 48 <= j < 64 and j % 2 == 0:
                    sched_ln_tile(g + 1, (j - 48) // 2)

        prev_job_end = [job_end[NJ - 2], job_end[NJ - 1]]
        prev_epi = [epi_done[NJ - 2], epi_done[NJ - 1]]

    S.emit()

    es1.close()

    es23 = ExitStack()
    yT = [es23.enter_context(nc.sbuf_tensor(f"yT{c}", [128, OWN], BF16)) for c in range(NC)]

    es2 = ExitStack()
    k_sb = es2.enter_context(nc.sbuf_tensor("ksb", [128, TB], BF16))
    vT_sb = es2.enter_context(nc.sbuf_tensor("vTsb", [128, TB], BF16))
    v_sb = [es2.enter_context(nc.sbuf_tensor(f"vsb{i}", [128, TB], BF16)) for i in range(2)]
    q_sb = [es2.enter_context(nc.sbuf_tensor(f"qsb{i}", [128, OWN], BF16)) for i in range(2)]
    p_sb = [es2.enter_context(nc.sbuf_tensor(f"psb{i}", [128, 1024], BF16)) for i in range(4)]
    mq_sb = es2.enter_context(nc.sbuf_tensor("mqsb", [128, 4 * 512], BF16))
    av_sb = [es2.enter_context(nc.sbuf_tensor(f"avsb{i}", [10, 128], BF16)) for i in range(2)]
    pa_sb = es2.enter_context(nc.sbuf_tensor("pasb", [10, 512], BF16))
    pa2_sb = es2.enter_context(nc.sbuf_tensor("pa2sb", [10, 512], BF16))
    bc_sb = es2.enter_context(nc.sbuf_tensor("bcsb", [128, 512], BF16))
    mone = es2.enter_context(nc.sbuf_tensor("mone", [128, 1], F32))
    rs_sb = [es2.enter_context(nc.sbuf_tensor(f"rssb{i}", [1, OWN], F32)) for i in range(2)]
    as_sb = es2.enter_context(nc.sbuf_tensor("assb", [1, 512], F32))

    scp = [es2.enter_context(nc.psum_tensor(f"scp{i}", [128, 1024], F32)) for i in range(2)]
    y_ps = es2.enter_context(nc.psum_tensor("y_ps", [128, 512], F32))
    rs_ps = es2.enter_context(nc.psum_tensor("rs_ps", [1, 512], F32))
    aux_ps = es2.enter_context(nc.psum_tensor("aux_ps", [128, 512], F32))
    vt_ps = es2.enter_context(nc.psum_tensor("vt_ps", [128, 512], BF16))

    S.new_block()
    S.fence_dma()
    S.dma("gpsimd", "mql", mq_sb[:], d_mq.rearrange("m p t -> p m t"))
    for og in range(8):
        S.dma("gpsimd", "wpc", d_wpbf[og],
              d_wpT.rearrange("(c p) o -> p c o", p=128)[:, :, og * 512:(og + 1) * 512])
    for og in range(8):
        S.dma("gpsimd", "wfc", d_wfbf[og],
              d_wfT.rearrange("(c p) o -> p c o", p=128)[:, :, og * 512:(og + 1) * 512])

    S.w("stkq0", S.n("stkq0")).w("stkq1", S.n("stkq1"))
    S.w("stxnf", S.n("stxnf"))
    S.w("stln0", S.n("stln0")).w("stln1", S.n("stln1"))
    S.op("sync", lambda q: q.nop())

    st = {"vtr_pe_last": 0, "vtr_ev": [[], []], "prev_exp": None, "prev_pv": None,
          "prev_yevac": None, "prev_bc": None, "sc_last": {}, "mask_first": True,
          "prev_chain_q": None, "prev_chain_bc": None, "prev_chain_yT": None,
          "prev_chain_done": None, "pidx": 0, "pv_hist": [None, None, None, None],
          "yev": [None, None], "chain_last_vv": [None, None],
          "chain_last_pe": [None, None], "rsinv": [None, None],
          "vtr_prev_ev": None, "vtr_pending": None, "chain_steps": []}

    def load_head_k(h):
        S.dma("sync", "kld", k_sb[:], d_k[h, :, :])

    def load_head_v(h):
        S.w("tensor", st["vtr_pe_last"])
        S.dma("sync", "vTld", vT_sb[:], d_v[h, :, :])

    def vtrans_round(h, r):
        if r < 8:
            S.w("vTld", 16 * (h + 1))
            if st["vtr_prev_ev"] is not None:
                S.w("vv", st["vtr_prev_ev"])

            def tfn(q, r=r):
                last = None
                for b4 in range(4):
                    ell = r * 4 + b4
                    last = q.transpose(vt_ps[:, b4 * 128:(b4 + 1) * 128],
                                       vT_sb[:, ell * 128:(ell + 1) * 128], idb[:])
                return last
            tend = S.op("tensor", tfn, inc="tensor")
            S.w("tensor", tend)
            ev = S.op("vector", lambda q, h=h, r=r: q.tensor_copy(
                v_sb[h % 2][:, r * 512:(r + 1) * 512], vt_ps[:]), inc="vv")
            st["vtr_prev_ev"] = ev
            st["vtr_ev"][h % 2].append(ev)
        else:
            S.w("vv", st["vtr_prev_ev"])
            tav = S.op("tensor", lambda q, h=h: q.transpose(
                vt_ps[0:10, 0:128], akav[:, (32 + h) * 10:(33 + h) * 10], idb[:]),
                inc="tensor")
            S.w("tensor", tav)
            if st["chain_last_pe"][h % 2] is not None:
                S.w("tensor", st["chain_last_pe"][h % 2])
            st["vtr_ev"][h % 2].append(S.op("vector", lambda q, h=h: q.tensor_copy(
                av_sb[h % 2][:, :], vt_ps[0:10, 0:128]), inc="vv"))
            st["vtr_pe_last"] = tav

    def sched_vtrans(h):
        st["vtr_ev"][h % 2] = []
        for r in range(9):
            vtrans_round(h, r)

    load_head_k(0)
    load_head_v(0)
    S.dma("sync", "qld", q_sb[0][:], d_q[0, :, :])
    sched_vtrans(0)
    S.op("vector", lambda q: q.memset(mone[:], -1.0), inc="vv")

    def enqueue_chain(h):
        par = h % 2
        cst = {"last_bc": st["prev_chain_bc"]}

        for g in range(NG):
            def s1(g=g, par=par, h=h):
                S.w("qld", 16 * (h + 1))
                if cst["last_bc"] is not None:
                    S.w("vv", cst["last_bc"])
                cst["as"] = S.op("tensor", lambda q, h=h, g=g, par=par: q.matmul(
                    aux_ps[0:10, :], akav[:, h * 10:(h + 1) * 10],
                    q_sb[par][:, g * 512:(g + 1) * 512], start=True, stop=True),
                    inc="tensor")
                st["prev_chain_q"] = cst["as"]

            def s2():
                S.w("tensor", cst["as"])
                cst["pae"] = S.op("scalar", lambda q: q.activation(
                    pa_sb[:], aux_ps[0:10, :], AF.Exp, scale=ccol[0:10, 2:3]), inc="aa")

            def s3():
                S.w("aa", cst["pae"])
                cst["asum"] = S.op("tensor", lambda q: q.matmul(
                    aux_ps[0:1, :], ones_b[0:10, 0:1], pa_sb[:],
                    start=True, stop=True), inc="tensor")

            def s4():
                S.w("tensor", cst["asum"])
                n_lna = S.op("scalar", lambda q: q.activation(
                    as_sb[:], aux_ps[0:1, :], AF.Ln), inc="aa")
                S.w("aa", n_lna)
                cst["inva"] = S.op("scalar", lambda q: q.activation(
                    as_sb[:], as_sb[:], AF.Exp, scale=mone[0:1, 0:1]), inc="aa")

            def s5():
                S.w("aa", cst["inva"])
                if cst["last_bc"] is not None:
                    S.w("vv", cst["last_bc"])
                cst["f2"] = S.op("vector", lambda q: q.tensor_scalar_mul(
                    as_sb[:], as_sb[:], gat_sb[0:1, 0:1]), inc="vv")

            def s6():
                S.w("vv", cst["f2"])
                cst["bc10"] = S.op("tensor", lambda q: q.matmul(
                    aux_ps[0:10, :], ones_f[0:1, 0:10], as_sb[:],
                    start=True, stop=True), inc="tensor")

            def s7():
                S.w("tensor", cst["bc10"])
                cst["pa2"] = S.op("vector", lambda q: q.tensor_mul(
                    pa2_sb[:], pa_sb[:], aux_ps[0:10, :]), inc="vv")

            def s8(g=g, par=par):
                S.w("vv", cst["pa2"])
                if st["rsinv"][par] is not None and st["rsinv"][par][g] is not None:
                    S.w("aa", st["rsinv"][par][g])
                cst["bc128"] = S.op("tensor", lambda q, par=par, g=g: q.matmul(
                    aux_ps[:], ones_f[0:1, 0:128], rs_sb[par][0:1, g * 512:(g + 1) * 512],
                    start=True, stop=True), inc="tensor")

            def s9():
                S.w("tensor", cst["bc128"])
                cst["bcc"] = S.op("vector", lambda q: q.tensor_copy(bc_sb[:], aux_ps[:]),
                                  inc="vv")

            def s10(g=g, par=par, h=h):
                S.w("vv", cst["bcc"])
                cst["ya"] = S.op("tensor", lambda q, par=par: q.matmul(
                    aux_ps[:], av_sb[par][:, :], pa2_sb[:], start=True, stop=True),
                    inc="tensor")
                S.w("vv", st["yev"][par][g]).w("vv", cst["bcc"])
                S.op("vector", lambda q, h=h, g=g: q.tensor_mul(
                    yT[h][:, g * 512:(g + 1) * 512],
                    yT[h][:, g * 512:(g + 1) * 512], bc_sb[:]), inc="vv")

            def s11(h=h, g=g, par=par):
                S.w("vv").w("tensor", cst["ya"])
                last_yT = S.op("vector", lambda q, h=h, g=g: q.tensor_add(
                    yT[h][:, g * 512:(g + 1) * 512],
                    yT[h][:, g * 512:(g + 1) * 512], aux_ps[:]), inc="vv")
                cst["last_bc"] = last_yT
                st["prev_chain_bc"] = last_yT
                st["chain_last_vv"][par] = last_yT
                st["chain_last_pe"][par] = S.n("tensor")

            st["chain_steps"] += [s1, s2, s3, s4, s5, s6, s7, s8, s9, s10, s11]

    def drain_chain():
        while st["chain_steps"]:
            st["chain_steps"].pop(0)()

    for h in range(H):
        par = h % 2
        qld_target = 16 * (h + 1)
        if h + 1 < H:
            load_head_v(h + 1)
            st["vtr_ev"][(h + 1) % 2] = []
            st["vtr_pending"] = [h + 1, 0]
        if h >= 1:
            enqueue_chain(h - 1)
        yev = [None] * NG
        rsev = [None] * NG
        rsinv = [None] * NG
        par4s = {}
        for g in range(NG):
            P = 4 * g + 4
            exp_done = [None] * P
            pv_done = [None] * P
            sc_done = [None] * P

            def emit_pv(p, P=P, par=par):
                par4 = par4s[p]
                S.w("aa", exp_done[p])
                if p == 0:
                    if st["prev_yevac"] is not None:
                        S.w("vv", st["prev_yevac"])
                    for ev in st["vtr_ev"][par]:
                        S.w("vv", ev)

                def pvfn(q, p=p, par4=par4, P=P, par=par):
                    for b2 in range(2):
                        q.matmul(rs_ps[:], ones_b[:, 0:1],
                                 p_sb[par4][:, b2 * 512:(b2 + 1) * 512],
                                 start=(p == 0 and b2 == 0), stop=(p == P - 1 and b2 == 1))
                    last = None
                    for b2 in range(2):
                        ell = 2 * p + b2
                        last = q.matmul(y_ps[:], v_sb[par][:, ell * 128:(ell + 1) * 128],
                                        p_sb[par4][:, b2 * 512:(b2 + 1) * 512],
                                        start=(p == 0 and b2 == 0), stop=(p == P - 1 and b2 == 1))
                    return last
                pv_done[p] = S.op("tensor", pvfn, inc="tensor")
                st["pv_hist"][par4] = pv_done[p]
                if st["vtr_pending"] is not None:
                    vh, vr = st["vtr_pending"]
                    if vr < 8:
                        vtrans_round(vh, vr)
                        st["vtr_pending"] = [vh, vr + 1]
                if st["chain_steps"]:
                    st["chain_steps"].pop(0)()

            for p in range(P):
                par2 = p % 2
                par4 = (st["pidx"] % 4)
                par4s[p] = par4
                st["pidx"] += 1
                S.w("kld", 16 * (h + 1))
                S.w("qld", qld_target)
                if p >= 2:
                    S.w("aa", exp_done[p - 2])
                elif st["prev_exp"] is not None:
                    S.w("aa", st["prev_exp"])

                def scfn(q, p=p, par2=par2, g=g, par=par):
                    last = None
                    for b2 in range(2):
                        ell = 2 * p + b2
                        last = q.matmul(scp[par2][:, b2 * 512:(b2 + 1) * 512],
                                        k_sb[:, ell * 128:(ell + 1) * 128],
                                        q_sb[par][:, g * 512:(g + 1) * 512],
                                        start=True, stop=True)
                    return last
                sc_done[p] = S.op("tensor", scfn, inc="tensor")

                own_pair = (4 * g <= p < 4 * g + 2)
                partner_pair = (p >= 4 * g + 2)
                mend = None
                if own_pair:
                    s0 = 2 * (p - 4 * g)
                    S.w("tensor", sc_done[p])
                    if st["mask_first"]:
                        S.w("mql", 16)
                        st["mask_first"] = False
                    S.op("vector", lambda q, par2=par2, s0=s0: q.tensor_add(
                        scp[par2][:, 0:512], scp[par2][:, 0:512],
                        mq_sb[:, s0 * 512:(s0 + 1) * 512]), inc="vv")
                    mend = S.op("vector", lambda q, par2=par2, s0=s0: q.tensor_add(
                        scp[par2][:, 512:1024], scp[par2][:, 512:1024],
                        mq_sb[:, (s0 + 1) * 512:(s0 + 2) * 512]), inc="vv")
                if own_pair:
                    S.w("vv", mend)
                else:
                    S.w("tensor", sc_done[p])
                if st["pv_hist"][par4] is not None:
                    S.w("tensor", st["pv_hist"][par4])
                bias_ap = hc_sb[:, 0:1] if partner_pair else ccol[:, 0:1]
                exp_done[p] = S.op("scalar", lambda q, par4=par4, par2=par2, b=bias_ap: q.activation(
                    p_sb[par4][:], scp[par2][:], AF.Exp, bias=b, scale=ccol[:, 1:2]),
                    inc="aa")
                if p >= 2:
                    emit_pv(p - 2)
            emit_pv(P - 2)
            emit_pv(P - 1)

            st["prev_exp"] = exp_done[P - 1]
            st["prev_pv"] = pv_done[P - 1]
            st["sc_last"][h] = sc_done[P - 1]

            S.w("tensor", pv_done[P - 1])
            yev[g] = S.op("vector", lambda q, h=h, g=g: q.tensor_copy(
                yT[h][:, g * 512:(g + 1) * 512], y_ps[:]), inc="vv")
            S.w("vv", yev[g])
            if g == 0 and st["chain_last_pe"][par] is not None:
                S.w("tensor", st["chain_last_pe"][par])
            if g == 0 and st["chain_last_vv"][par] is not None:
                S.w("vv", st["chain_last_vv"][par])
            rsev[g] = S.op("vector", lambda q, par=par, g=g: q.tensor_copy(
                rs_sb[par][0:1, g * 512:(g + 1) * 512], rs_ps[:]), inc="vv")
            st["prev_yevac"] = rsev[g]
        st["yev"][par] = yev
        for g in range(NG):
            S.w("vv", rsev[g])
            S.op("scalar", lambda q, par=par, g=g: q.activation(
                rs_sb[par][0:1, g * 512:(g + 1) * 512],
                rs_sb[par][0:1, g * 512:(g + 1) * 512], AF.Ln), inc="aa")
            S.w("aa")
            rsinv[g] = S.op("scalar", lambda q, par=par, g=g: q.activation(
                rs_sb[par][0:1, g * 512:(g + 1) * 512],
                rs_sb[par][0:1, g * 512:(g + 1) * 512], AF.Exp,
                scale=mone[0:1, 0:1]), inc="aa")
        st["rsinv"][par] = rsinv

        drain_chain()

        if h + 1 < H:
            S.w("tensor", st["sc_last"][h])
            load_head_k(h + 1)
            S.w("qld", 16 * (h + 1))
            if st.get("prev_chain_q") is not None:
                S.w("tensor", st["prev_chain_q"])
            S.dma("sync", "qld", q_sb[(h + 1) % 2][:], d_q[h + 1, :, :])
            if st["vtr_pending"] is not None:
                vh, vr = st["vtr_pending"]
                for r in range(vr, 9):
                    vtrans_round(vh, r)
                st["vtr_pending"] = None

    enqueue_chain(H - 1)
    drain_chain()

    S.emit()
    es2.close()

    es3a = ExitStack()
    wp_sb = es3a.enter_context(nc.sbuf_tensor("wp_sb", [128, 32 * 512], BF16))
    x3 = [es3a.enter_context(nc.sbuf_tensor(f"x3{i}", [128, 512], F32)) for i in range(4)]
    o3 = [es3a.enter_context(nc.sbuf_tensor(f"o3{i}", [128, 512], F32)) for i in range(4)]
    pp = [es3a.enter_context(nc.psum_tensor(f"pp{i}", [128, 512], F32)) for i in range(4)]

    S.new_block()
    S.fence_dma()

    def wp_src(og, hh):
        return d_wpT.rearrange("(c p) o -> p c o", p=128)[
            :, hh * 16:(hh + 1) * 16, og * 512:(og + 1) * 512]

    NT3 = 16

    def x_rows(tb):
        g = tb // 4
        r0 = g * GW + (tb % 4) * 128
        return d_xp[r0:r0 + 128, :]

    tb_end = {}
    ep_end = {}
    stc = {}
    S.w("wpc", S.n("wpc"))
    S.dma("sync", "wp0", wp_sb[:, 0:16 * 512], d_wpbf[0][:, 0:16 * 512])
    S.dma("sync", "wp1", wp_sb[:, 16 * 512:], d_wpbf[0][:, 16 * 512:])
    for tb in range(4):
        S.dma("sync", f"x3l{tb}", x3[tb][:], x_rows(tb)[:, 0:512])
    xcnt = [16, 16, 16, 16]

    for og in range(8):
        for tb in range(NT3):
            slot = tb % 4
            idx = og * NT3 + tb
            S.w("wp0", 16 * (og + 1))
            if idx >= 4:
                S.w("vv", ep_end[idx - 4])

            def fn_a(q, slot=slot, tb=tb):
                last = None
                for cc in range(16):
                    last = q.matmul(pp[slot][:], yT[cc][:, tb * 128:(tb + 1) * 128],
                                    wp_sb[:, cc * 512:(cc + 1) * 512],
                                    start=(cc == 0), stop=False)
                return last
            S.op("tensor", fn_a, inc="tensor")
            S.w("wp1", 16 * (og + 1))

            def fn_b(q, slot=slot, tb=tb):
                last = None
                for cc in range(16, NC):
                    last = q.matmul(pp[slot][:], yT[cc][:, tb * 128:(tb + 1) * 128],
                                    wp_sb[:, cc * 512:(cc + 1) * 512],
                                    start=False, stop=(cc == NC - 1))
                return last
            tb_end[idx] = S.op("tensor", fn_b, inc="tensor")

            S.w("tensor", tb_end[idx])
            S.w(f"x3l{slot}", xcnt[slot])
            S.w(f"st3{slot}", stc.get(idx - 4, 0))
            ep_end[idx] = S.op("vector", lambda q, slot=slot: q.tensor_add(
                o3[slot][:], pp[slot][:], x3[slot][:]), inc="vv")
            S.w("vv", ep_end[idx])
            S.dma("scalar", f"st3{slot}",
                  d_acc[tb * 128:(tb + 1) * 128, og * 512:(og + 1) * 512], o3[slot][:])
            stc[idx] = S.n(f"st3{slot}")

            nidx = idx + 4
            if nidx < 8 * NT3:
                nog, ntb = nidx // NT3, nidx % NT3
                S.w("vv", ep_end[idx])
                S.dma("sync", f"x3l{slot}", x3[slot][:],
                      x_rows(ntb)[:, nog * 512:(nog + 1) * 512])
                xcnt[slot] += 16

        if og + 1 < 8:
            S.w("tensor", tb_end[og * NT3 + NT3 - 1])
            S.dma("sync", "wp0", wp_sb[:, 0:16 * 512], d_wpbf[og + 1][:, 0:16 * 512])
            S.dma("sync", "wp1", wp_sb[:, 16 * 512:], d_wpbf[og + 1][:, 16 * 512:])

    S.emit()
    es3a.close()
    es23.close()

    es3b = ExitStack()
    xn3 = [es3b.enter_context(nc.sbuf_tensor(f"xn3{c}", [128, OWN], BF16))
           for c in range(NC)]
    wf_sb = es3b.enter_context(nc.sbuf_tensor("wf_sb", [128, 32 * 512], BF16))
    bt_sb = es3b.enter_context(nc.sbuf_tensor("btsb", [64, C], BF16))
    at_sb = es3b.enter_context(nc.sbuf_tensor("atsb", [128, NC * 64], BF16))
    gw_sb = es3b.enter_context(nc.sbuf_tensor("gwsb", [128, NC * 4], BF16))
    rkm_sb = es3b.enter_context(nc.sbuf_tensor("rkmsb", [4, 64], BF16))
    gbbc_sb = es3b.enter_context(nc.sbuf_tensor("gbbcsb", [128, 4], F32))
    epsio_sb = es3b.enter_context(nc.sbuf_tensor("epsiosb", [128, 4], F32))
    bbc_sb = [es3b.enter_context(nc.sbuf_tensor(f"bbcsb{i}", [128, 512], BF16))
              for i in range(2)]
    gs_sb = es3b.enter_context(nc.sbuf_tensor("gssb", [128, 4], F32))
    gm_sb = es3b.enter_context(nc.sbuf_tensor("gmsb", [128, 1], F32))
    sel_sb = es3b.enter_context(nc.sbuf_tensor("selsb", [128, 4], BF16))
    selT_sb = es3b.enter_context(nc.sbuf_tensor("selTsb", [4, OWN], BF16))
    u_sb = es3b.enter_context(nc.sbuf_tensor("usb", [64, 512], F32))
    mu_sb = es3b.enter_context(nc.sbuf_tensor("musb", [64, OWN], BF16))
    acc3 = [es3b.enter_context(nc.sbuf_tensor(f"acc3{i}", [128, 512], F32))
            for i in range(2)]
    ob3 = [es3b.enter_context(nc.sbuf_tensor(f"ob3{i}", [128, 512], F32))
           for i in range(2)]
    gps = es3b.enter_context(nc.psum_tensor("gps", [128, 4], F32))
    stp = es3b.enter_context(nc.psum_tensor("stp", [4, 128], BF16))
    ups = es3b.enter_context(nc.psum_tensor("ups", [64, 512], F32))
    smp = es3b.enter_context(nc.psum_tensor("smp", [64, 512], F32))
    pf = [es3b.enter_context(nc.psum_tensor(f"pf{i}", [128, 512], F32)) for i in range(4)]

    S.new_block()
    S.fence_dma()
    S.dma("gpsimd", "ld3b", bt_sb[:], d_BT[:])
    S.dma("gpsimd", "ld3b", at_sb[:], d_AT.rearrange("(c p) r -> p c r", p=128))
    S.dma("gpsimd", "ld3b", gw_sb[:], d_gwT.rearrange("(c p) e -> p c e", p=128))
    S.dma("gpsimd", "ld3b", rkm_sb[:], d_rkm[:])

    S.dma("sync", "ld3s", gbbc_sb[:], d_gbbc[:])
    S.dma("sync", "ld3s", epsio_sb[:], d_epsio[:])
    S.w("stxnf", S.n("stxnf"))
    for cchunk in range(NC):
        S.dma("sync", "xn3l", xn3[cchunk][:], d_xnf[cchunk, :, :])
    ld3b_all = S.n("ld3b")
    ld3s_all = S.n("ld3s")
    xn3_all = S.n("xn3l")

    n_sel = None
    n_selT = None
    n_mu = None
    for Q in range(4):
        for tbq in range(4):
            tb = Q * 4 + tbq
            S.w("ld3b", ld3b_all).w("xn3l", xn3_all)
            if n_sel is not None:
                S.w("vv", n_sel)

            def gfn(q, tb=tb):
                last = None
                for cc in range(NC):
                    last = q.matmul(gps[:], xn3[cc][:, tb * 128:(tb + 1) * 128],
                                    gw_sb[:, cc * 4:(cc + 1) * 4],
                                    start=(cc == 0), stop=(cc == NC - 1))
                return last
            n_g = S.op("tensor", gfn, inc="tensor")
            S.w("tensor", n_g).w("ld3s", ld3s_all)
            S.op("vector", lambda q: q.tensor_add(gs_sb[:], gps[:], gbbc_sb[:]), inc="vv")
            S.w("vv")
            S.op("vector", lambda q: q.tensor_sub(gs_sb[:], gs_sb[:], epsio_sb[:]), inc="vv")
            S.w("vv")
            S.op("vector", lambda q: q.reduce_max(gm_sb[:], gs_sb[:], axis=AX.X), inc="vv")
            S.w("vv")
            if n_selT is not None:
                S.w("tensor", prev_nt)
            n_sel = S.op("vector", lambda q: q.tensor_single_scalar(
                sel_sb[:], gs_sb[:], gm_sb[:, 0:1], ALU.is_equal), inc="vv")
            S.w("vv", n_sel)
            if n_selT is not None:
                S.w("vv", n_selT)
            n_t = S.op("tensor", lambda q: q.transpose(stp[:], sel_sb[:], idb[:]),
                       inc="tensor")
            prev_nt = n_t
            S.w("tensor", n_t)
            n_selT = S.op("vector", lambda q, tb=tb: q.tensor_copy(
                selT_sb[:, tb * 128:(tb + 1) * 128], stp[:]), inc="vv")

        S.w("ld3b", ld3b_all).w("xn3l", xn3_all)
        if n_mu is not None:
            S.w("vv", n_mu)

        def ufn(q, Q=Q):
            last = None
            for cc in range(NC):
                last = q.matmul(ups[:], at_sb[:, cc * 64:(cc + 1) * 64],
                                xn3[cc][:, Q * 512:(Q + 1) * 512],
                                start=(cc == 0), stop=(cc == NC - 1))
            return last
        n_u = S.op("tensor", ufn, inc="tensor")
        S.w("tensor", n_u)
        n_ue = S.op("vector", lambda q: q.tensor_copy(u_sb[:], ups[:]), inc="vv")
        S.w("vv", n_selT)
        if n_mu is not None:
            S.w("vv", n_mu)
        n_sm = S.op("tensor", lambda q, Q=Q: q.matmul(
            smp[:], rkm_sb[:], selT_sb[:, Q * 512:(Q + 1) * 512],
            start=True, stop=True), inc="tensor")
        S.w("tensor", n_sm).w("vv", n_ue)
        n_mu = S.op("vector", lambda q, Q=Q: q.tensor_mul(
            mu_sb[:, Q * 512:(Q + 1) * 512], u_sb[:], smp[:]), inc="vv")

    def wf_src(og, hh):
        return d_wfT.rearrange("(c p) o -> p c o", p=128)[
            :, hh * 16:(hh + 1) * 16, og * 512:(og + 1) * 512]

    S.w("wfc", S.n("wfc"))
    S.dma("sync", "wf0", wf_sb[:, 0:16 * 512], d_wfbf[0][:, 0:16 * 512])
    S.dma("sync", "wf1", wf_sb[:, 16 * 512:], d_wfbf[0][:, 16 * 512:])
    S.dma("gpsimd", "bbcl0", bbc_sb[0][:], d_bbc[:, 0:512])
    S.dma("gpsimd", "bbcl1", bbc_sb[1][:], d_bbc[:, 512:1024])
    for slot in range(4):
        S.w(f"st3{slot}", S.n(f"st3{slot}"))
    for tb in range(2):
        S.dma("sync", f"ac{tb}", acc3[tb][:], d_acc[tb * 128:(tb + 1) * 128, 0:512])
    accnt = [16, 16]
    tb_end = {}
    ep_end = {}
    stc = {}
    for og in range(8):
        for tb in range(NT3):
            slot = tb % 4
            sl2 = tb % 2
            idx = og * NT3 + tb
            S.w("wf0", 16 * (og + 1))
            S.w("vv", n_mu)
            if idx >= 4:
                S.w("vv", ep_end[idx - 4])

            def fn_a(q, slot=slot, tb=tb):
                last = None
                for cc in range(16):
                    last = q.matmul(pf[slot][:], xn3[cc][:, tb * 128:(tb + 1) * 128],
                                    wf_sb[:, cc * 512:(cc + 1) * 512],
                                    start=(cc == 0), stop=False)
                return last
            S.op("tensor", fn_a, inc="tensor")
            S.w("wf1", 16 * (og + 1))

            def fn_b(q, slot=slot, tb=tb, og=og):
                last = None
                for cc in range(16, NC):
                    last = q.matmul(pf[slot][:], xn3[cc][:, tb * 128:(tb + 1) * 128],
                                    wf_sb[:, cc * 512:(cc + 1) * 512],
                                    start=False, stop=False)
                last = q.matmul(pf[slot][:], mu_sb[:, tb * 128:(tb + 1) * 128],
                                bt_sb[:, og * 512:(og + 1) * 512],
                                start=False, stop=True)
                return last
            tb_end[idx] = S.op("tensor", fn_b, inc="tensor")

            S.w("tensor", tb_end[idx])
            S.w(f"ac{sl2}", accnt[sl2])
            S.w(f"bbcl{og % 2}", 16 * (og // 2 + 1))
            if idx == 0:
                S.w("ld3b", ld3b_all)
            if idx >= 2:
                S.w(f"so{sl2}", stc.get(idx - 2, 0))
            S.op("vector", lambda q, slot=slot, sl2=sl2: q.tensor_add(
                ob3[sl2][:], pf[slot][:], acc3[sl2][:]), inc="vv")
            S.w("vv")
            ep_end[idx] = S.op("vector", lambda q, sl2=sl2, og=og: q.tensor_add(
                ob3[sl2][:], ob3[sl2][:], bbc_sb[og % 2][:]), inc="vv")
            S.w("vv", ep_end[idx])
            S.dma("scalar", f"so{sl2}",
                  d_out[tb * 128:(tb + 1) * 128, og * 512:(og + 1) * 512], ob3[sl2][:])
            stc[idx] = S.n(f"so{sl2}")

            nidx = idx + 2
            if nidx < 8 * NT3:
                nog, ntb = nidx // NT3, nidx % NT3
                S.w("vv", ep_end[idx])
                S.dma("sync", f"ac{sl2}", acc3[sl2][:],
                      d_acc[ntb * 128:(ntb + 1) * 128, nog * 512:(nog + 1) * 512])
                accnt[sl2] += 16

        if og + 1 < 8:
            S.w("tensor", tb_end[og * NT3 + NT3 - 1])
            S.dma("sync", "wf0", wf_sb[:, 0:16 * 512], d_wfbf[og + 1][:, 0:16 * 512])
            S.dma("sync", "wf1", wf_sb[:, 16 * 512:], d_wfbf[og + 1][:, 16 * 512:])
            if og + 2 < 8:
                S.w("vv", ep_end[og * NT3 + NT3 - 1])
                S.dma("gpsimd", f"bbcl{og % 2}", bbc_sb[og % 2][:],
                      d_bbc[:, (og + 2) * 512:(og + 3) * 512])

    for slot in range(2):
        S.w(f"so{slot}", S.n(f"so{slot}"))
    S.op("scalar", lambda q: q.nop())
    S.emit()
    es3b.close()
    es0.close()
    return nc



def _tables(h2):
    pos = np.empty(TB, np.int64)
    for g in range(NG):
        own = 2 * g + h2
        other = 2 * g + 1 - h2
        pos[g * GW:g * GW + 512] = np.arange(own * 512, own * 512 + 512)
        pos[g * GW + 512:(g + 1) * GW] = np.arange(other * 512, other * 512 + 512)
    theta = 1.0 / (10000.0 ** (np.arange(0, 128, 2, dtype=np.float64) / 128.0))
    ang = pos[None, :] * theta[:, None]
    cos = np.cos(ang).astype(np.float32)
    sin = np.sin(ang).astype(np.float32)
    cosT = np.concatenate([cos, cos], 0)
    srtT = np.concatenate([sin, -sin], 0)
    return (np.ascontiguousarray(cosT).astype(ml_dtypes.bfloat16),
            np.ascontiguousarray(srtT).astype(ml_dtypes.bfloat16))


def _maskq():
    m = np.zeros((4, 128, 512), np.float32)
    kk, rr = np.meshgrid(np.arange(128), np.arange(128), indexing="ij")
    tri = np.where(kk <= rr, 0.0, NEG).astype(np.float32)
    for s in range(4):
        for j in range(4):
            if j < s:
                m[s, :, j * 128:(j + 1) * 128] = NEG
            elif j == s:
                m[s, :, j * 128:(j + 1) * 128] = tri
    return m


def kernel(L0_attn_x, L0_attn_w_attn, L0_attn_w_proj, L0_attn_adapter_wte,
           L0_attn_gating, L0_ffn_W, L0_ffn_b, L0_ffn_A, L0_ffn_B,
           L0_ffn_gate_w, L0_ffn_gate_b):
    x = np.asarray(L0_attn_x)
    wqkT = np.ascontiguousarray(np.asarray(L0_attn_w_attn).T)
    wpT = np.ascontiguousarray(np.asarray(L0_attn_w_proj).T)
    wfT = np.ascontiguousarray(np.asarray(L0_ffn_W).T)
    awT = np.ascontiguousarray(np.asarray(L0_attn_adapter_wte).T)
    AT = np.ascontiguousarray(np.asarray(L0_ffn_A).T)
    BT = np.ascontiguousarray(np.asarray(L0_ffn_B).T)
    gwT = np.ascontiguousarray(np.asarray(L0_ffn_gate_w).T)
    gating = np.ascontiguousarray(np.asarray(L0_attn_gating).reshape(1, 1).astype(np.float32))
    bbc = np.ascontiguousarray(np.broadcast_to(
        np.asarray(L0_ffn_b).astype(np.float32)[None, :], (128, C)))
    gbbc = np.ascontiguousarray(np.broadcast_to(
        np.asarray(L0_ffn_gate_b).astype(np.float32)[None, :], (128, 4)))
    epsio = np.ascontiguousarray(np.broadcast_to(
        (np.arange(4, dtype=np.float32) * 1e-6)[None, :], (128, 4)))
    rkm = np.zeros((4, 64), np.float32)
    for e, r in enumerate((8, 16, 32, 64)):
        rkm[e, :r] = 1.0
    mq = _maskq()
    ident = np.eye(128, dtype=np.float32)

    in_maps = []
    for core in range(8):
        b, h2 = core // 2, core % 2
        xb = np.asarray(x[b])
        xp = np.empty((TB, C), np.float32)
        for g in range(NG):
            own = 2 * g + h2
            other = 2 * g + 1 - h2
            xp[g * GW:g * GW + 512] = xb[own * 512:(own + 1) * 512]
            xp[g * GW + 512:(g + 1) * GW] = xb[other * 512:(other + 1) * 512]
        cosT, srtT = _tables(h2)
        hcm = np.zeros((128, 2), np.float32)
        hcm[:, 0] = EXPB + (NEG * S128 if h2 == 0 else 0.0)
        in_maps.append({
            "xp": xp, "wqkT": wqkT, "wpT": wpT, "wfT": wfT, "awT": awT,
            "AT": AT, "BTm": BT, "gwT": gwT, "cosT": cosT, "srtT": srtT,
            "mq": mq, "hc": hcm, "gating": gating, "gbbc": gbbc,
            "epsio": epsio, "rkm": rkm, "bbc": bbc, "identf": ident,
        })

    nc = build_program()
    res = run_bass_kernel_spmd(nc, in_maps, core_ids=list(range(8)))

    out = np.empty((4, TB, C), np.float32)
    for core in range(8):
        b, h2 = core // 2, core % 2
        oc = res.results[core]["out"]
        for g in range(NG):
            own = 2 * g + h2
            out[b, own * 512:(own + 1) * 512] = oc[g * 512:(g + 1) * 512]
    return out
